# version 19
# baseline (speedup 1.0000x reference)
import numpy as np
import ml_dtypes
import concourse.bass as bass
import concourse.mybir as mybir
from concourse.bass_utils import run_bass_kernel_spmd
from contextlib import ExitStack

F32 = mybir.dt.float32
BF16 = mybir.dt.bfloat16
AF = mybir.ActivationFunctionType
ALU = mybir.AluOpType

D = 2048
T = 2048
NG = 16
NM = 16
NCORES = 8
EPS_LN = 1e-5
EPS_FFN = 1e-7
RD = 1.0 / D
P0_STOP = None
PK_MODE = None
SKIP_P0 = False
PK_NOSPILL = False
NO_EPI = False
EPI_MODE = "full"
DBG_P0 = False
PACE = {"pk", "pv", "att", "ffn", "cwk", "cwr", "cwv"}

ENGINES = ("sync", "gpsimd", "vector", "scalar", "tensor")
PVN = ["ln1_w", "ln1_b", "ln2_w", "ln2_b", "u", "w", "mk", "mv", "mr",
       "bo", "cmk", "cmr", "scale"]
DVN = ["UW", "UPW", "TW2", "OMK", "OMV", "OMR",
       "CK1", "CK0", "CKB", "CR1", "CR0", "CRB",
       "MK1", "MK0", "MKB", "MV1", "MV0", "MVB", "MR1", "MR0", "MRB"]


class Emitter:
    def sem(self, name):
        self.sems.append(name)
        self.counts[name] = 0

    def count(self, name):
        return self.counts[name]

    class _Op:
        __slots__ = ("fn", "incs")

        def __init__(self, fn):
            self.fn = fn
            self.incs = []

        def __call__(self, eng, sems):
            r = self.fn(eng)
            for sem_name, n in self.incs:
                r.then_inc(sems[sem_name], n)

    def __init__(self):
        self.plan = {e: [] for e in ENGINES}
        self.counts = {}
        self.sems = []
        self.dummy_fns = {}

    def op(self, engine, fn, inc=None):
        entry = Emitter._Op(fn)
        if inc is not None:
            sem_name, n = inc
            self.counts[sem_name] += n
            entry.incs.append((sem_name, n))
        self.plan[engine].append(entry)
        return self.counts[inc[0]] if inc is not None else None

    def wait(self, engine, sem_name, val):
        if val is None or val <= 0:
            return

        def wrapped(eng, sems, sem_name=sem_name, val=val):
            eng.wait_ge(sems[sem_name], val)
        self.plan[engine].append(wrapped)

    def mark(self, engine, sem_name):
        assert engine != "tensor"
        self.counts[sem_name] += 1
        last = self.plan[engine][-1] if self.plan[engine] else None
        if isinstance(last, Emitter._Op) and not last.incs:
            last.incs.append((sem_name, 1))
        else:
            entry = Emitter._Op(self.dummy_fns[engine])
            entry.incs.append((sem_name, 1))
            self.plan[engine].append(entry)
        return self.counts[sem_name]

    def replay(self, nc):
        with ExitStack() as ctx:
            sems = {n: ctx.enter_context(nc.semaphore(n)) for n in self.sems}
            with nc.Block() as block:
                for name in ENGINES:
                    ops = self.plan[name]
                    if not ops:
                        continue
                    getattr(block, name)(
                        lambda eng, ops=ops, sems=sems: [f(eng, sems) for f in ops])


class KB:
    def __init__(self, dump=False):
        nc = bass.Bass("TRN2", target_bir_lowering=False,
                       detect_race_conditions=False)
        self.nc = nc
        E = Emitter()
        self.E = E
        for s in ("LD", "GW", "TE", "EV", "VE", "SE", "GE", "DONE",
                  "W0", "W1", "GK", "GN", "GV", "GX", "GL",
                  "LB0", "LB1", "LW", "LV"):
            E.sem(s)
        dt = nc.dram_tensor
        self.xt = dt("xt", [D, T], F32, kind="ExternalInput").ap()
        self.w_h = {}
        for wn in ("wk", "wv", "wr", "wo", "cwk", "cwv", "cwr", "wf"):
            self.w_h[wn] = dt(wn, [D, D], F32, kind="ExternalInput").ap()
        self.pv = dt("pv", [128, len(PVN) * 16], F32, kind="ExternalInput").ap()
        self.out_h = dt("out", [D, T], F32, kind="ExternalOutput").ap()
        ik = "ExternalOutput" if dump else "Internal"
        self.xn32_h = dt("xn32_h", [D, T], F32, kind=ik).ap()
        self.xnb_h = dt("xnb_h", [D, T], BF16, kind=ik).ap()
        self.kb_h = dt("kb_h", [D, T], BF16, kind=ik).ap()
        self.vb_h = dt("vb_h", [D, T], BF16, kind=ik).ap()
        self.x2b_h = dt("x2b_h", [D, T], BF16, kind=ik).ap()
        self.out1_h = dt("out1_h", [D, T], F32, kind=ik).ap()
        self.fb_h = dt("fb_h", [D, T], BF16, kind=ik).ap()
        self.k2b_h = dt("k2b_h", [D, T], BF16, kind=ik).ap()
        self.cmb_h = dt("cmb_h", [D, T], BF16, kind=ik).ap()
        self.wkv_h = dt("wkv_h", [D, T], BF16, kind=ik).ap()
        self.ab_h = dt("ab_h", [128, T], F32, kind=ik).ap()
        if DBG_P0:
            self.dbg1 = dt("dbg1", [128, T], F32, kind="ExternalOutput").ap()
            self.dbg2 = dt("dbg2", [128, T], F32, kind="ExternalOutput").ap()
            self.dbg3 = dt("dbg3", [128, T], F32, kind="ExternalOutput").ap()
        self.gi = 0
        self.te_mark = {}
        self.sl = 0
        self.sl_mark = {}
        self.sl_group = {}

    def alloc(self, ctx):
        nc = self.nc
        sb = lambda n, s, d: ctx.enter_context(nc.sbuf_tensor(n, s, d))
        self.X = sb("X", [128, NG * T], BF16)
        self.Y = sb("Y", [128, NG * T], BF16)
        self.A = [sb(f"A{i}", [128, T], F32) for i in range(7)]
        self.KR = [sb(f"KR{i}", [128, T], BF16) for i in range(2)]
        self.VB = [sb(f"VB{i}", [128, T], BF16) for i in range(1)]
        self.WSL = [sb(f"WSL{i}", [128, NG * 128], BF16) for i in range(2)]
        self.PV = sb("PV", [128, len(PVN) * 16], F32)
        self.DV = sb("DV", [128, len(DVN) * 16], F32)
        self.ONES = sb("ONES", [128, 128], BF16)
        self.EPSC = sb("EPSC", [128, 2], F32)
        self.WUC = sb("WUC", [128, 16], F32)
        self.ps = [ctx.enter_context(nc.psum_tensor(f"ps{i}", [128, 512], F32))
                   for i in range(8)]
        self.E.dummy_fns = {
            "vector": lambda e: e.memset(self.WUC[:, 2:3], 0.0),
            "scalar": lambda e: e.copy(out=self.WUC[:, 3:4], in_=self.EPSC[:, 1:2]),
            "gpsimd": lambda e: e.memset(self.WUC[:, 4:5], 0.0),
        }

    def p(self, name, g):
        i = PVN.index(name)
        return self.PV[:, i * 16 + g: i * 16 + g + 1]

    def dv(self, name, g):
        i = DVN.index(name)
        return self.DV[:, i * 16 + g: i * 16 + g + 1]

    def Xg(self, g, lo=0, hi=T):
        return self.X[:, g * T + lo: g * T + hi]

    def Yg(self, g, lo=0, hi=T):
        return self.Y[:, g * T + lo: g * T + hi]

    def ht(self, h, g):
        return h[g * 128:(g + 1) * 128, :]

    def init_consts(self):
        E = self.E
        E.op("sync", lambda e: e.dma_start(out=self.PV[:], in_=self.pv[:, :]),
             inc=("LD", 16))
        pv_ld = E.count("LD")
        E.op("vector", lambda e: e.memset(self.ONES[:], 1.0))
        E.op("vector", lambda e: e.memset(self.EPSC[:, 0:1], EPS_LN))
        E.op("vector", lambda e: e.memset(self.EPSC[:, 1:2], 1.0))
        E.op("vector", lambda e: e.memset(self.WUC[:], 1.0))
        E.wait("vector", "LD", pv_ld)
        pvi = {n: i for i, n in enumerate(PVN)}
        dvi = {n: i for i, n in enumerate(DVN)}

        def pvs(n):
            i = pvi[n]
            return self.PV[:, i * 16:(i + 1) * 16]

        def dvs(n):
            i = dvi[n]
            return self.DV[:, i * 16:(i + 1) * 16]
        E.op("vector", lambda e: e.tensor_tensor(
            out=dvs("UW"), in0=pvs("u"), in1=pvs("w"), op=ALU.subtract))
        E.op("vector", lambda e: e.tensor_tensor(
            out=dvs("UPW"), in0=pvs("u"), in1=pvs("w"), op=ALU.add))
        E.op("vector", lambda e: e.tensor_scalar(
            out=dvs("TW2"), in0=pvs("w"), scalar1=2.0, scalar2=None,
            op0=ALU.mult, op1=ALU.bypass))
        for dst, src in (("OMK", "mk"), ("OMV", "mv"), ("OMR", "mr")):
            E.op("vector", lambda e, dst=dst, src=src: e.tensor_scalar(
                out=dvs(dst), in0=pvs(src), scalar1=-1.0, scalar2=1.0,
                op0=ALU.mult, op1=ALU.add))
        for pre, mx in (("MK", "mk"), ("MV", "mv"), ("MR", "mr")):
            E.op("vector", lambda e, pre=pre, mx=mx: e.tensor_tensor(
                out=dvs(pre + "1"), in0=pvs(mx), in1=pvs("ln1_w"), op=ALU.mult))
            E.op("vector", lambda e, pre=pre, mx=mx: e.tensor_tensor(
                out=dvs(pre + "B"), in0=pvs(mx), in1=pvs("ln1_b"), op=ALU.mult))
            E.op("vector", lambda e, pre=pre: e.tensor_tensor(
                out=dvs(pre + "0"), in0=pvs("ln1_w"), in1=dvs(pre + "1"),
                op=ALU.subtract))
        for pre, mx in (("CK", "cmk"), ("CR", "cmr")):
            E.op("vector", lambda e, pre=pre, mx=mx: e.tensor_tensor(
                out=dvs(pre + "1"), in0=pvs(mx), in1=pvs("ln2_w"), op=ALU.mult))
            E.op("vector", lambda e, pre=pre, mx=mx: e.tensor_tensor(
                out=dvs(pre + "B"), in0=pvs(mx), in1=pvs("ln2_b"), op=ALU.mult))
            E.op("vector", lambda e, pre=pre: e.tensor_tensor(
                out=dvs(pre + "0"), in0=pvs("ln2_w"), in1=dvs(pre + "1"),
                op=ALU.subtract))
        self.dv_ve = E.mark("vector", "VE")
        E.wait("scalar", "VE", self.dv_ve)
        for i in range(24):
            E.op("scalar", lambda e: e.activation(
                out=self.WUC[:, 0:1], in_=self.EPSC[:, 1:2], func=AF.Exp, scale=0.0))
            E.op("scalar", lambda e: e.activation(
                out=self.WUC[:, 1:2], in_=self.EPSC[:, 1:2], func=AF.Ln, scale=0.0,
                bias=self.EPSC[:, 1:2]))

    def new_group(self):
        gi = self.gi
        self.gi += 1
        banks = [self.ps[(gi % 2) * 4 + n] for n in range(4)]
        return gi, banks

    def mm_group(self, lhsT_fn, rhs_fn, pre_waits=(), per_g_waits=None):
        E = self.E
        gi, banks = self.new_group()
        if gi >= 2:
            E.wait("tensor", "EV", gi - 1)
        for sem, val in pre_waits:
            E.wait("tensor", sem, val)
        for g in range(NG):
            if per_g_waits is not None:
                for sem, val in per_g_waits(g):
                    E.wait("tensor", sem, val)
            for n in range(4):
                last = (g == NG - 1 and n == 3)
                E.op("tensor", lambda e, g=g, n=n, lhsT_fn=lhsT_fn, rhs_fn=rhs_fn, banks=banks: e.matmul(
                    banks[n][:, :], lhsT_fn(g), rhs_fn(g, n),
                    start=(g == 0), stop=(g == NG - 1)),
                    inc=(("TE", 1) if last else None))
        self.te_mark[gi] = E.count("TE")
        return gi, banks

    def evac_pre(self, engine, gi):
        E = self.E
        E.wait(engine, "TE", self.te_mark[gi])
        E.wait(engine, "EV", gi)

    def evac_post(self, engine):
        return self.E.mark(engine, "EV")

    def wsl_load(self, w_ap, m):
        E = self.E
        s = self.sl
        if s >= 2:
            gprev = self.sl_group.get(s - 2)
            if gprev is not None:
                E.wait("gpsimd", "TE", self.te_mark[gprev])
        wr = w_ap.rearrange("(g p) c -> p g c", p=128)
        dst = self.WSL[s % 2][:, :].rearrange("p (g c) -> p g c", c=128)
        wsem = "W%d" % (s % 2)
        E.op("gpsimd", lambda e, wr=wr, dst=dst, m=m: e.dma_start(
            out=dst, in_=wr[:, :, m * 128:(m + 1) * 128]), inc=(wsem, 16))
        self.sl_mark[s] = (wsem, E.count(wsem))
        self.sl += 1
        return s

    def weight_phase(self, wname, rhs_fn, evac_fn, rhs_waits=(), hook=None,
                     m0_per_g=None, pre_slices=None):
        w_ap = self.w_h[wname]
        E = self.E
        slices = (list(pre_slices) if pre_slices is not None else
                  [self.wsl_load(w_ap, 0), self.wsl_load(w_ap, 1)])
        for m in range(NM):
            s = slices[m]
            pw = list(rhs_waits) if m == 0 else []
            pw.append(self.sl_mark[s])
            gi, banks = self.mm_group(
                lambda g, s=s: self.WSL[s % 2][:, g * 128:(g + 1) * 128],
                rhs_fn, pre_waits=pw,
                per_g_waits=(m0_per_g if m == 0 else None))
            self.sl_group[s] = gi
            if m + 2 < NM:
                slices.append(self.wsl_load(w_ap, m + 2))
            evac_fn(m, gi, banks)
            if hook is not None:
                hook(m)
        return self.E.count("TE")

    def stats_dual(self, rhs1_fn, rhs2_fn, ready_fn, mu_buf, ex2_buf):
        E = self.E
        gi = self.gi
        self.gi += 2
        if gi % 2 == 0:
            b1 = [self.ps[n] for n in range(4)]
            b2 = [self.ps[4 + n] for n in range(4)]
        else:
            b1 = [self.ps[4 + n] for n in range(4)]
            b2 = [self.ps[n] for n in range(4)]
        E.wait("tensor", "EV", gi)
        g_marks = {}
        for g in range(NG):
            rd = ready_fn(g)
            if rd is not None:
                if isinstance(rd, list):
                    for sem, val in rd:
                        E.wait("tensor", sem, val)
                else:
                    E.wait("tensor", rd[0], rd[1])
            for n in range(4):
                E.op("tensor", lambda e, g=g, n=n, b1=b1, rhs1_fn=rhs1_fn: e.matmul(
                    b1[n][:, :], self.ONES[:, :], rhs1_fn(g, n),
                    start=(g == 0), stop=(g == NG - 1)))
            for n in range(4):
                last = (n == 3)
                E.op("tensor", lambda e, g=g, n=n, b2=b2, rhs2_fn=rhs2_fn: e.matmul(
                    b2[n][:, :], self.ONES[:, :], rhs2_fn(g, n),
                    start=(g == 0), stop=(g == NG - 1)),
                    inc=(("TE", 1) if last else None))
            g_marks[g] = E.count("TE")
        self.te_mark[gi] = self.te_mark[gi + 1] = E.count("TE")
        self.evac_pre("vector", gi)
        for n in range(4):
            E.op("vector", lambda e, n=n, b1=b1, mu_buf=mu_buf: e.tensor_scalar(
                out=mu_buf[:, n * 512:(n + 1) * 512], in0=b1[n][:, :],
                scalar1=RD, scalar2=None, op0=ALU.mult, op1=ALU.bypass))
        for n in range(4):
            E.op("vector", lambda e, n=n, b2=b2, ex2_buf=ex2_buf: e.tensor_scalar(
                out=ex2_buf[:, n * 512:(n + 1) * 512], in0=b2[n][:, :],
                scalar1=RD, scalar2=None, op0=ALU.mult, op1=ALU.bypass))
        self.evac_post("vector")
        self.evac_post("vector")
        return g_marks

    def ln_finalize(self, mu, ex2, inv_out, t1, t2):
        E = self.E
        E.op("vector", lambda e: e.tensor_tensor(
            out=t1[:], in0=mu[:], in1=mu[:], op=ALU.mult))
        E.op("vector", lambda e: e.tensor_tensor(
            out=t2[:], in0=ex2[:], in1=t1[:], op=ALU.subtract))
        vm = E.mark("vector", "VE")
        E.wait("scalar", "VE", vm)
        E.op("scalar", lambda e: e.activation(
            out=t1[:], in_=t2[:], func=AF.Ln, bias=self.EPSC[:, 0:1], scale=1.0))
        E.op("scalar", lambda e: e.activation(
            out=inv_out[:], in_=t1[:], func=AF.Exp, scale=-0.5))
        return E.mark("scalar", "SE")

    def phase0(self):
        E = self.E
        A = self.A
        self.wk_pre = [self.wsl_load(self.w_h["wk"], 0),
                       self.wsl_load(self.w_h["wk"], 1)]
        ld1 = {}
        te_base = E.count("TE")
        for g in range(NG):
            if g >= 2:
                E.wait("sync", "VE", self.p1ve[g - 2]) if hasattr(self, "p1ve") else None
            if g >= 2:
                E.wait("sync", "TE", te_base + g - 1)
            qs = "LV" if g % 2 == 0 else "GW"
            E.op("sync", lambda e, g=g: e.dma_start(
                out=A[g % 2][:], in_=self.ht(self.xt, g)), inc=(qs, 16))
            ld1[g] = (qs, E.count(qs))
        self.p1ve = {}
        for g in range(NG):
            E.wait("vector", *ld1[g])
            if g >= 1:
                E.wait("vector", "TE", te_base + g)
            E.op("vector", lambda e, g=g: e.tensor_copy(
                out=self.VB[0][:], in_=A[g % 2][:]))
            E.op("vector", lambda e, g=g: e.tensor_tensor(
                out=self.KR[g % 2][:], in0=self.VB[0][:],
                in1=self.VB[0][:], op=ALU.mult))
            self.p1ve[g] = E.mark("vector", "VE")
        if P0_STOP == "pass1pre":
            return
        self.stats_dual(
            lambda g, n: self.VB[0][:, n * 512:(n + 1) * 512],
            lambda g, n: self.KR[g % 2][:, n * 512:(n + 1) * 512],
            lambda g: ("VE", self.p1ve[g]), A[3], A[6])
        if P0_STOP == "stats1":
            return
        se0 = self.ln_finalize(A[3], A[6], A[6], A[2], A[4])
        if P0_STOP == "fin0":
            return
        E.wait("vector", "SE", se0)
        ld2 = {}
        sp2 = {}
        te_base2 = E.count("TE")
        self.p2ve = {}

        def p2_load(g):
            if g < 2:
                E.wait("sync", "VE", self.p1ve[NG - 1])
            qs = "LV" if g % 2 == 0 else "GW"
            E.op("sync", lambda e, g=g: e.dma_start(
                out=A[g % 2][:], in_=self.ht(self.xt, g)), inc=(qs, 16))
            ld2[g] = (qs, E.count(qs))

        p2_load(0)
        p2_load(1)
        for g in range(NG):
            E.wait("vector", *ld2[g])
            if g < 2:
                E.wait("vector", "TE", te_base2)
            if g >= 2:
                E.wait("vector", "LD", sp2[g - 2])
                E.wait("vector", "TE", te_base2 + g - 1)
            E.op("vector", lambda e, g=g: e.tensor_tensor(
                out=A[2][:], in0=A[g % 2][:], in1=A[3][:], op=ALU.subtract))
            E.op("vector", lambda e, g=g: e.tensor_tensor(
                out=A[4 + g % 2][:], in0=A[2][:], in1=A[6][:], op=ALU.mult))
            E.op("vector", lambda e, g=g: e.tensor_copy(
                out=self.Xg(g), in_=A[4 + g % 2][:]))
            E.op("vector", lambda e, g=g: e.tensor_tensor(
                out=self.KR[g % 2][:], in0=self.Xg(g), in1=self.Xg(g), op=ALU.mult))
            self.p2ve[g] = E.mark("vector", "VE")
            E.wait("sync", "VE", self.p2ve[g])
            if g >= 1:
                E.wait("sync", "LD", sp2[g - 1])
            E.op("sync", lambda e, g=g: e.dma_start(
                out=self.ht(self.xn32_h, g), in_=A[4 + g % 2][:]), inc=("LD", 16))
            E.op("sync", lambda e, g=g: e.dma_start(
                out=self.ht(self.xnb_h, g), in_=self.Xg(g)), inc=("LD", 16))
            sp2[g] = E.count("LD")
            if g + 2 < NG:
                p2_load(g + 2)
        self.xnb_sp = sp2
        if P0_STOP == "pass2":
            return
        self.stats_dual(
            lambda g, n: self.Xg(g, n * 512, (n + 1) * 512),
            lambda g, n: self.KR[g % 2][:, n * 512:(n + 1) * 512],
            lambda g: ("VE", self.p2ve[g]), A[3], A[6])
        se1 = self.ln_finalize(A[3], A[6], A[6], A[2], A[4])
        if P0_STOP == "fin1":
            return
        self.se1 = se1
        self.sp2_last = sp2[NG - 1]

    def tm_mix(self, pre, wait_te=None, g_waits=None, first_waits=()):
        E = self.E
        A = self.A
        marks = {}
        for g in range(NG):
            if g == 0 and wait_te is not None:
                E.wait("vector", "TE", wait_te)
            if g == 0:
                for sem, val in first_waits:
                    E.wait("vector", sem, val)
            if g_waits is not None:
                E.wait("vector", *g_waits(g))
            sc = A[2 + g % 2]
            E.op("vector", lambda e, g=g, sc=sc: e.tensor_scalar(
                out=sc[:, 1:T], in0=self.Xg(g, 0, T - 1),
                scalar1=self.dv(pre + "0", g), scalar2=self.p("ln1_b", g),
                op0=ALU.mult, op1=ALU.add))
            E.op("vector", lambda e, g=g, sc=sc: e.tensor_scalar(
                out=sc[:, 0:1], in0=self.dv(pre + "B", g),
                scalar1=1.0, scalar2=None, op0=ALU.mult, op1=ALU.bypass))
            E.op("vector", lambda e, g=g, sc=sc: e.scalar_tensor_tensor(
                out=self.Yg(g), in0=self.Xg(g), scalar=self.dv(pre + "1", g),
                in1=sc[:], op0=ALU.mult, op1=ALU.add))
            marks[g] = E.mark("vector", "VE")
        return marks

    def evac_spill(self, spill_h, track, prior_ld=None):
        E = self.E

        def evac(m, gi, banks):
            self.evac_pre("scalar", gi)
            if m == 0 and prior_ld is not None:
                E.wait("scalar", *prior_ld())
            if m >= 2:
                E.wait("scalar", *track[m - 2])
            for n in range(4):
                E.op("scalar", lambda e, n=n, m=m, banks=banks: e.copy(
                    out=self.KR[m % 2][:, n * 512:(n + 1) * 512],
                    in_=banks[n][:, :]))
            self.evac_post("scalar")
            sm = E.mark("scalar", "SE")
            if PK_NOSPILL:
                track[m] = ("LD", 0)
                return
            E.wait("sync", "SE", sm)
            bsem = "LB%d" % (m % 2)
            E.op("sync", lambda e, m=m: e.dma_start(
                out=self.ht(spill_h, m), in_=self.KR[m % 2][:]), inc=(bsem, 16))
            track[m] = (bsem, E.count(bsem))
        return evac

    def phase_k(self):
        E = self.E
        if PK_MODE == "memsetY":
            for g in range(NG):
                self.E.op("vector", lambda e, g=g: e.memset(self.Yg(g), 0.25))
            self.kmix_ve = E.mark("vector", "VE")
            kmk = {g: self.kmix_ve for g in range(NG)}
        else:
            E.wait("vector", "SE", self.se1)
            E.wait("vector", "LD", self.sp2_last)
            E.wait("gpsimd", "SE", self.se1)
            E.wait("gpsimd", "LD", self.sp2_last)
            self.t2mk = {}
            kmk = {}
            for g in range(NG):
                if g >= 2:
                    E.wait("gpsimd", "VE", self.t2mk[g - 2])
                E.op("gpsimd", lambda e, g=g: e.tensor_tensor(
                    out=self.A[g % 2][:], in0=self.Xg(g), in1=self.A[3][:],
                    op=ALU.subtract))
                gm = E.mark("gpsimd", "GE")
                E.wait("vector", "GE", gm)
                E.op("vector", lambda e, g=g: e.tensor_tensor(
                    out=self.Xg(g), in0=self.A[g % 2][:], in1=self.A[6][:],
                    op=ALU.mult))
                self.t2mk[g] = E.mark("vector", "VE")
                sc = self.A[4 + g % 2]
                E.op("vector", lambda e, g=g, sc=sc: e.tensor_scalar(
                    out=sc[:, 1:T], in0=self.Xg(g, 0, T - 1),
                    scalar1=self.dv("MK0", g), scalar2=self.p("ln1_b", g),
                    op0=ALU.mult, op1=ALU.add))
                E.op("vector", lambda e, g=g, sc=sc: e.tensor_scalar(
                    out=sc[:, 0:1], in0=self.dv("MKB", g),
                    scalar1=1.0, scalar2=None, op0=ALU.mult, op1=ALU.bypass))
                E.op("vector", lambda e, g=g, sc=sc: e.scalar_tensor_tensor(
                    out=self.Yg(g), in0=self.Xg(g), scalar=self.dv("MK1", g),
                    in1=sc[:], op0=ALU.mult, op1=ALU.add))
                kmk[g] = E.mark("vector", "VE")
            self.a1b_ve = self.t2mk[NG - 1]
            self.kmix_ve = kmk[NG - 1]
        if PK_MODE == "mix":
            return
        self.ksp = {}
        self.pk_te = self.weight_phase(
            "wk", lambda g, n: self.Yg(g, n * 512, (n + 1) * 512),
            self.evac_spill(self.kb_h, self.ksp),
            m0_per_g=(lambda g: [("VE", kmk[g])]) if "pk" in PACE else None,
            rhs_waits=() if "pk" in PACE else (("VE", self.kmix_ve),),
            pre_slices=self.wk_pre)

    def phase_v(self):
        E = self.E
        A = self.A
        vmk = self.tm_mix("MV", wait_te=self.pk_te)
        self.vmix_ve = vmk[NG - 1]
        self.vevmk = {}

        def vev(m, gi, banks):
            self.evac_pre("scalar", gi)
            if m < 2:
                E.wait("scalar", *self.ksp[NM - 1])
            else:
                E.wait("scalar", "GE", self.M4[m - 2])
            for n in range(4):
                E.op("scalar", lambda e, n=n, m=m, banks=banks: e.copy(
                    out=self.KR[m % 2][:, n * 512:(n + 1) * 512],
                    in_=banks[n][:, :]))
            self.evac_post("scalar")
            self.vevmk[m] = E.mark("scalar", "SE")
        self.M1 = {}; self.M2 = {}; self.M3 = {}; self.M4 = {}
        self.M5 = {}; self.M6 = {}; self.M7 = {}; self.M8 = {}
        self.kld = {}; self.vld = {}; self.wkvsp = {}

        def S(g):
            b = 4 * (g % 2)
            return A[b], A[b + 1], A[b + 2], A[3]

        def emit_kb_load(g):
            s0, s1, s2, s3 = S(g)
            E.wait("gpsimd", *self.ksp[g])
            if g >= 1:
                E.wait("gpsimd", "GK", self.kld[g - 1])
            if g >= 2:
                E.wait("gpsimd", "SE", self.M6[g - 2])
            E.op("gpsimd", lambda e, g=g, s0=s0: e.dma_start(
                out=s0[:], in_=self.ht(self.kb_h, g)), inc=("GK", 16))
            self.kld[g] = E.count("GK")

        def emit_args(g):
            s0, s1, s2, s3 = S(g)
            E.wait("vector", "GK", self.kld[g])
            if g >= 2:
                E.wait("vector", "GE", self.M7[g - 2])
            E.op("vector", lambda e, g=g, s0=s0, s1=s1: e.tensor_scalar(
                out=s1[:], in0=s0[:], scalar1=self.dv("UW", g), scalar2=0.0,
                op0=ALU.add, op1=ALU.min))
            E.op("vector", lambda e, g=g, s0=s0, s2=s2: e.tensor_scalar(
                out=s2[:], in0=s0[:], scalar1=self.dv("UPW", g),
                scalar2=self.dv("TW2", g), op0=ALU.add, op1=ALU.max))
            self.M1[g] = E.mark("vector", "VE")

        def emit_exps(g):
            s0, s1, s2, s3 = S(g)
            E.wait("scalar", "VE", self.M1[g])
            if g >= 1:
                E.wait("scalar", "VE", self.M5[g - 1])
            E.op("scalar", lambda e, s2=s2, s3=s3: e.activation(
                out=s3[:], in_=s2[:], func=AF.Exp, scale=-1.0))
            E.op("scalar", lambda e, s1=s1, s2=s2: e.activation(
                out=s2[:], in_=s1[:], func=AF.Exp, scale=1.0))
            self.M2[g] = E.mark("scalar", "SE")

        def emit_b(g):
            s0, s1, s2, s3 = S(g)
            E.wait("vector", "SE", self.M2[g])
            E.op("vector", lambda e, s1=s1, s2=s2, s3=s3: e.tensor_tensor_scan(
                out=s1[:], data0=s3[:], data1=s2[:], initial=0.0,
                op0=ALU.mult, op1=ALU.add))
            self.M3[g] = E.mark("vector", "VE")

        def emit_e2v(g):
            s0, s1, s2, s3 = S(g)
            E.wait("gpsimd", "SE", self.M2[g])
            E.wait("gpsimd", "VE", self.M1[g])
            E.wait("gpsimd", "SE", self.vevmk[g])
            E.op("gpsimd", lambda e, s0=s0, s2=s2, g=g: e.tensor_tensor(
                out=s0[:], in0=s2[:], in1=self.KR[g % 2][:], op=ALU.mult))
            self.M4[g] = E.mark("gpsimd", "GE")

        def emit_a(g):
            s0, s1, s2, s3 = S(g)
            E.wait("vector", "GE", self.M4[g])
            E.op("vector", lambda e, s0=s0, s2=s2, s3=s3: e.tensor_tensor_scan(
                out=s2[:], data0=s3[:], data1=s0[:], initial=0.0,
                op0=ALU.mult, op1=ALU.add))
            self.M5[g] = E.mark("vector", "VE")

        def emit_binv(g):
            s0, s1, s2, s3 = S(g)
            E.wait("scalar", "VE", self.M5[g])
            E.op("scalar", lambda e, s0=s0, s1=s1: e.activation(
                out=s0[:], in_=s1[:], func=AF.Ln, scale=1.0))
            E.op("scalar", lambda e, s0=s0, s1=s1: e.activation(
                out=s1[:], in_=s0[:], func=AF.Exp, scale=-1.0))
            self.M6[g] = E.mark("scalar", "SE")

        def emit_wkv(g):
            s0, s1, s2, s3 = S(g)
            E.wait("gpsimd", "VE", self.M5[g])
            E.wait("gpsimd", "SE", self.M6[g])
            if g >= 1:
                E.wait("gpsimd", "LW", self.wkvsp[g - 1])
            E.op("gpsimd", lambda e, s1=s1, s2=s2, g=g: e.tensor_tensor(
                out=self.VB[0][:], in0=s2[:], in1=s1[:], op=ALU.mult))
            self.M7[g] = E.mark("gpsimd", "GE")
            E.wait("sync", "GE", self.M7[g])
            E.op("sync", lambda e, g=g: e.dma_start(
                out=self.ht(self.wkv_h, g), in_=self.VB[0][:]), inc=("LW", 16))
            self.wkvsp[g] = E.count("LW")

        def hook(m):
            if m >= 2:
                emit_wkv(m - 2)
            emit_kb_load(m)
            if m >= 1:
                emit_b(m - 1)
                emit_e2v(m - 1)
                emit_a(m - 1)
                emit_binv(m - 1)
            emit_args(m)
            emit_exps(m)

        self.pv_te = self.weight_phase(
            "wv", lambda g, n: self.Yg(g, n * 512, (n + 1) * 512),
            vev, m0_per_g=(lambda g: [("VE", vmk[g])]) if "pv" in PACE else None,
            rhs_waits=() if "pv" in PACE else (("VE", self.vmix_ve),), hook=hook)
        emit_wkv(NG - 2)
        emit_b(NG - 1)
        emit_e2v(NG - 1)
        emit_a(NG - 1)
        emit_binv(NG - 1)
        emit_wkv(NG - 1)

    def mix_sigmoid(self, src_g_fn, dst_g_fn, m_name, om_name, first_waits=()):
        E = self.E
        A = self.A
        VH = {}
        S1 = {}
        VD = {}
        SF = {}

        def v_head(g):
            sb_ = A[g % 3]
            mb_ = A[3 + g % 3]
            if g >= 3:
                E.wait("vector", "SE", SF[g - 3])
            E.op("vector", lambda e, g=g, sb_=sb_: e.tensor_scalar(
                out=sb_[:, 1:T], in0=src_g_fn(g, 0, T - 1),
                scalar1=self.dv(m_name + "0", g), scalar2=self.p("ln1_b", g),
                op0=ALU.mult, op1=ALU.add))
            E.op("vector", lambda e, g=g, sb_=sb_: e.tensor_scalar(
                out=sb_[:, 0:1], in0=self.dv(m_name + "B", g),
                scalar1=1.0, scalar2=None, op0=ALU.mult, op1=ALU.bypass))
            E.op("vector", lambda e, g=g, sb_=sb_, mb_=mb_: e.scalar_tensor_tensor(
                out=mb_[:], in0=src_g_fn(g, 0, T), scalar=self.dv(m_name + "1", g),
                in1=sb_[:], op0=ALU.mult, op1=ALU.add))
            VH[g] = E.mark("vector", "VE")

        def s_e(g):
            sb_ = A[g % 3]
            mb_ = A[3 + g % 3]
            E.wait("scalar", "VE", VH[g])
            E.op("scalar", lambda e, sb_=sb_, mb_=mb_: e.activation(
                out=sb_[:], in_=mb_[:], func=AF.Exp, scale=-1.0))
            S1[g] = E.mark("scalar", "SE")

        def v_d(g):
            sb_ = A[g % 3]
            mb_ = A[3 + g % 3]
            E.wait("vector", "SE", S1[g])
            E.op("vector", lambda e, sb_=sb_, mb_=mb_: e.tensor_scalar(
                out=mb_[:], in0=sb_[:], scalar1=1.0, scalar2=None,
                op0=ALU.add, op1=ALU.bypass))
            VD[g] = E.mark("vector", "VE")

        def s_fin(g):
            sb_ = A[g % 3]
            mb_ = A[3 + g % 3]
            E.wait("scalar", "VE", VD[g])
            E.op("scalar", lambda e, sb_=sb_, mb_=mb_: e.activation(
                out=sb_[:], in_=mb_[:], func=AF.Ln, scale=1.0))
            E.op("scalar", lambda e, g=g, sb_=sb_: e.activation(
                out=dst_g_fn(g), in_=sb_[:], func=AF.Exp, scale=-1.0))
            SF[g] = E.mark("scalar", "SE")

        for eng, sem, val in first_waits:
            E.wait(eng, sem, val)
        for g in range(NG):
            v_head(g)
            s_e(g)
            if g >= 1:
                v_d(g - 1)
            if g >= 2:
                s_fin(g - 2)
        v_d(NG - 1)
        s_fin(NG - 2)
        s_fin(NG - 1)
        return SF

    def rmix_sigmoid(self):
        fw = (("vector", "TE", self.pv_te),
              ("vector", "GE", self.M7[NG - 1]),
              ("scalar", "GE", self.M7[NG - 1]))
        return self.mix_sigmoid(
            lambda g, lo, hi: self.Xg(g, lo, hi),
            lambda g: self.Yg(g), "MR", "OMR", first_waits=fw)

    def phase_r(self):
        E = self.E
        A = self.A
        rmix_sf = self.rmix_sigmoid()
        self.rev = {}
        wkld = {}

        def evac_r(m, gi, banks):
            if m == 0:
                E.wait("gpsimd", "LW", self.wkvsp[NM - 1])
            else:
                E.wait("gpsimd", "LW", self.wkvsp[m])
                E.wait("gpsimd", "VE", self.M8[m - 1])
            E.op("gpsimd", lambda e, m=m: e.dma_start(
                out=self.VB[0][:], in_=self.ht(self.wkv_h, m)), inc=("GV", 16))
            wkld[m] = E.count("GV")
            self.evac_pre("scalar", gi)
            if m == 0:
                E.wait("scalar", "GE", self.M4[NM - 1])
            if m >= 2:
                E.wait("scalar", "VE", self.M8[m - 2])
            for n in range(4):
                E.op("scalar", lambda e, n=n, m=m, banks=banks: e.copy(
                    out=self.KR[m % 2][:, n * 512:(n + 1) * 512],
                    in_=banks[n][:, :]))
            self.evac_post("scalar")
            self.rev[m] = E.mark("scalar", "SE")
            E.wait("vector", "GV", wkld[m])
            E.wait("vector", "SE", self.rev[m])
            E.op("vector", lambda e, m=m: e.tensor_tensor(
                out=self.Xg(m), in0=self.KR[m % 2][:], in1=self.VB[0][:],
                op=ALU.mult))
            self.M8[m] = E.mark("vector", "VE")

        self.pr_te = self.weight_phase(
            "wr", lambda g, n: self.Yg(g, n * 512, (n + 1) * 512),
            evac_r, m0_per_g=lambda g: [("SE", rmix_sf[g])])
        self.wkvr_done = self.M8[NG - 1]

    def phase_att(self):
        E = self.E
        A = self.A
        self.xnb_gw = {}

        def xnb_load(m):
            if m == 0:
                E.wait("gpsimd", "TE", self.pr_te)
            else:
                E.wait("gpsimd", "GN", self.xnb_gw[m - 1])
            E.wait("gpsimd", "LD", self.xnb_sp[m])
            E.op("gpsimd", lambda e, m=m: e.dma_start(
                out=self.Yg(m), in_=self.ht(self.xnb_h, m)), inc=("GN", 16))
            self.xnb_gw[m] = E.count("GN")

        ld = {}
        ve_base = E.count("VE")
        self.att_ve_pre = lambda m: ve_base + 2 * (m + 1)

        def att_load(m):
            if m == 0:
                E.wait("sync", "SE", self.rev[NM - 1])
                E.wait("sync", "VE", self.wkvr_done)
            if m >= 2:
                E.wait("sync", "VE", self.att_ve_pre(m - 2))
            qs = "LV" if m % 2 == 0 else "GW"
            E.op("sync", lambda e, m=m: e.dma_start(
                out=A[m % 2][:], in_=self.ht(self.xn32_h, m)), inc=(qs, 16))
            E.op("sync", lambda e, m=m: e.dma_start(
                out=A[2 + m % 2][:], in_=self.ht(self.xt, m)), inc=(qs, 16))
            ld[m] = (qs, E.count(qs))

        att_load(0)
        att_load(1)
        sp = {}
        x2cp = {}

        def evac_att(m, gi, banks):
            self.evac_pre("vector", gi)
            E.wait("vector", *ld[m])
            if m >= 2:
                E.wait("vector", "LD", sp[m - 2])
            if m >= 1:
                E.wait("vector", "SE", x2cp[m - 1])
            t2 = A[4]
            for n in range(4):
                E.op("vector", lambda e, n=n, m=m, banks=banks, t2=t2: e.scalar_tensor_tensor(
                    out=t2[:, n * 512:(n + 1) * 512], in0=banks[n][:, :],
                    scalar=self.p("bo", m), in1=A[m % 2][:, n * 512:(n + 1) * 512],
                    op0=ALU.add, op1=ALU.add))
            self.evac_post("vector")
            ob = A[5 + m % 2]
            E.op("vector", lambda e, m=m, t2=t2, ob=ob: e.tensor_tensor(
                out=ob[:], in0=t2[:], in1=A[2 + m % 2][:], op=ALU.add))
            vm1 = E.mark("vector", "VE")
            E.wait("scalar", "VE", vm1)
            if m >= 2:
                E.wait("scalar", "LD", sp[m - 2])
            E.op("scalar", lambda e, m=m, t2=t2: e.activation(
                out=self.KR[m % 2][:], in_=t2[:], func=AF.Copy, scale=1.0))
            sm = E.mark("scalar", "SE")
            x2cp[m] = sm
            vm2 = E.mark("vector", "VE")
            assert vm2 == self.att_ve_pre(m), (vm2, self.att_ve_pre(m))
            E.wait("sync", "SE", sm)
            if m >= 1:
                E.wait("sync", "LD", sp[m - 1])
            E.op("sync", lambda e, m=m: e.dma_start(
                out=self.ht(self.x2b_h, m), in_=self.KR[m % 2][:]), inc=("LD", 16))
            E.wait("sync", "VE", vm1)
            E.op("sync", lambda e, m=m, ob=ob: e.dma_start(
                out=self.ht(self.out1_h, m), in_=ob[:]), inc=("LD", 16))
            sp[m] = E.count("LD")
            if m + 2 < NM:
                att_load(m + 2)
            else:
                E.wait("sync", "LD", sp[m])

        self.wf_pre = None

        def att_hook(m):
            xnb_load(m)
            if m == NM - 1:
                self.wf_pre = [self.wsl_load(self.w_h["wf"], 0),
                               self.wsl_load(self.w_h["wf"], 1)]

        self.patt_te = self.weight_phase(
            "wo", lambda g, n: self.Xg(g, n * 512, (n + 1) * 512),
            evac_att,
            m0_per_g=(lambda g: [("VE", self.M8[g])]) if "att" in PACE else None,
            rhs_waits=() if "att" in PACE else (("VE", self.wkvr_done),),
            hook=att_hook)
        self.ffn_rhs_gw = self.xnb_gw[NM - 1]
        self.x2sp = sp

    def phase_ffn(self):
        E = self.E
        A = self.A
        self.fsp = {}
        self.f2ve = {}

        def evac_f(m, gi, banks):
            self.evac_pre("scalar", gi)
            if m == 0:
                E.wait("scalar", "LD", self.x2sp[NM - 1])
            if m >= 2:
                E.wait("scalar", "LD", self.fsp[m - 2])
            for n in range(4):
                E.op("scalar", lambda e, n=n, m=m, banks=banks: e.activation(
                    out=self.KR[m % 2][:, n * 512:(n + 1) * 512],
                    in_=banks[n][:, :], func=AF.Copy, scale=self.p("scale", m)))
            self.evac_post("scalar")
            sm = E.mark("scalar", "SE")
            E.wait("sync", "SE", sm)
            E.op("sync", lambda e, m=m: e.dma_start(
                out=self.ht(self.fb_h, m), in_=self.KR[m % 2][:]), inc=("LD", 16))
            self.fsp[m] = E.count("LD")
            E.wait("sync", "LD", self.fsp[m])
            E.wait("vector", "SE", sm)
            if m == 0:
                E.wait("vector", "TE", self.patt_te)
            E.op("vector", lambda e, m=m: e.tensor_tensor(
                out=self.Xg(m), in0=self.KR[m % 2][:], in1=self.KR[m % 2][:],
                op=ALU.mult))
            self.f2ve[m] = E.mark("vector", "VE")

        self.pffn_te = self.weight_phase(
            "wf", lambda g, n: self.Yg(g, n * 512, (n + 1) * 512),
            evac_f,
            m0_per_g=(lambda g: [("GN", self.xnb_gw[g])]) if "ffn" in PACE else None,
            rhs_waits=() if "ffn" in PACE else (("GN", self.ffn_rhs_gw),),
            pre_slices=self.wf_pre)

        gi, banks = self.new_group()
        E.wait("tensor", "EV", gi - 1)
        for m in range(NM):
            E.wait("tensor", "VE", self.f2ve[m])
            for n in range(4):
                last = (m == NM - 1 and n == 3)
                E.op("tensor", lambda e, m=m, n=n, banks=banks: e.matmul(
                    banks[n][:, :], self.ONES[:, :],
                    self.Xg(m, n * 512, (n + 1) * 512),
                    start=(m == 0), stop=(m == NM - 1)),
                    inc=(("TE", 1) if last else None))
        self.te_mark[gi] = E.count("TE")
        self.evac_pre("vector", gi)
        for n in range(4):
            E.op("vector", lambda e, n=n, banks=banks: e.tensor_scalar(
                out=A[0][:, n * 512:(n + 1) * 512], in0=banks[n][:, :],
                scalar1=1.0, scalar2=None, op0=ALU.mult, op1=ALU.bypass))
        self.evac_post("vector")
        vm0 = E.mark("vector", "VE")
        E.wait("scalar", "VE", vm0)
        E.op("scalar", lambda e: e.activation(out=A[1][:], in_=A[0][:], func=AF.Ln, scale=1.0))
        E.op("scalar", lambda e: e.activation(out=A[0][:], in_=A[1][:], func=AF.Exp, scale=0.5))
        sm1 = E.mark("scalar", "SE")
        VM = {}

        def sch(k, vops, sops):
            def run(k=k, vops=vops, sops=sops):
                for f in vops:
                    f()
                VM[k] = E.mark("vector", "VE")
                if sops is not None:
                    E.wait("scalar", "VE", VM[k])
                    for f in sops:
                        f()
                    VM[(k, "s")] = E.mark("scalar", "SE")
            return run

        ab_v_steps = [
            sch(1, [lambda: E.wait("vector", "SE", sm1),
                    lambda: E.op("vector", lambda e: e.tensor_scalar(
                        out=A[1][:], in0=A[0][:], scalar1=EPS_FFN, scalar2=EPS_FFN,
                        op0=ALU.max, op1=ALU.add))],
                [lambda: E.op("scalar", lambda e: e.activation(out=A[2][:], in_=A[1][:], func=AF.Ln, scale=1.0)),
                 lambda: E.op("scalar", lambda e: e.activation(out=A[1][:], in_=A[2][:], func=AF.Exp, scale=-1.0))]),
            sch(2, [lambda: E.wait("vector", "SE", VM[(1, "s")]),
                    lambda: E.op("vector", lambda e: e.tensor_scalar(
                        out=A[2][:], in0=A[0][:], scalar1=20.0, scalar2=2.0,
                        op0=ALU.min, op1=ALU.mult))],
                [lambda: E.op("scalar", lambda e: e.activation(out=A[3][:], in_=A[2][:], func=AF.Exp, scale=1.0))]),
            sch(3, [lambda: E.wait("vector", "SE", VM[(2, "s")]),
                    lambda: E.op("vector", lambda e: e.tensor_scalar(
                        out=A[2][:], in0=A[3][:], scalar1=1.0, scalar2=None,
                        op0=ALU.add, op1=ALU.bypass))],
                [lambda: E.op("scalar", lambda e: e.activation(out=A[3][:], in_=A[2][:], func=AF.Ln, scale=1.0)),
                 lambda: E.op("scalar", lambda e: e.activation(out=A[2][:], in_=A[3][:], func=AF.Exp, scale=-1.0))]),
            sch(4, [lambda: E.wait("vector", "SE", VM[(3, "s")]),
                    lambda: E.op("vector", lambda e: e.tensor_scalar(
                        out=A[3][:], in0=A[2][:], scalar1=-2.0, scalar2=1.0,
                        op0=ALU.mult, op1=ALU.add)),
                    lambda: E.op("vector", lambda e: e.tensor_tensor(
                        out=A[2][:], in0=A[3][:], in1=A[1][:], op=ALU.mult)),
                    lambda: E.op("vector", lambda e: e.tensor_tensor(
                        out=A[1][:], in0=A[2][:], in1=A[0][:], op=ALU.mult)),
                    lambda: E.op("vector", lambda e: e.tensor_scalar(
                        out=A[0][:], in0=A[1][:], scalar1=EPS_FFN, scalar2=None,
                        op0=ALU.max, op1=ALU.bypass))],
                [lambda: E.op("scalar", lambda e: e.activation(out=A[1][:], in_=A[0][:], func=AF.Ln, scale=1.0)),
                 lambda: E.op("scalar", lambda e: e.activation(out=A[0][:], in_=A[1][:], func=AF.Exp, scale=-1.0))]),
            sch(5, [lambda: E.wait("vector", "SE", VM[(4, "s")]),
                    lambda: E.op("vector", lambda e: e.tensor_scalar(
                        out=A[1][:], in0=A[0][:], scalar1=0.99, scalar2=1.0,
                        op0=ALU.mult, op1=ALU.min)),
                    lambda: E.op("vector", lambda e: e.tensor_tensor(
                        out=A[3][:], in0=A[2][:], in1=A[1][:], op=ALU.mult))],
                None),
        ]

        x2ld = {}
        sqmk = {}
        tes_base = E.count("TE")
        stat_mark = lambda g: tes_base + g + 1

        def x2_load(g):
            bs = "LB%d" % (g % 2)
            if g >= 2:
                E.wait("sync", "VE", sqmk[g - 2])
                E.wait("sync", "TE", stat_mark(g - 2))
            E.op("sync", lambda e, g=g: e.dma_start(
                out=self.KR[g % 2][:], in_=self.ht(self.x2b_h, g)),
                inc=(bs, 16))
            x2ld[g] = (bs, E.count(bs))

        x2_load(0)
        x2_load(1)
        abv_i = 0
        for g in range(NG):
            E.wait("vector", *x2ld[g])
            if g == 0:
                E.wait("vector", "VE", self.wkvr_done)
            if g >= 1:
                E.wait("vector", "TE", stat_mark(g - 1))
            E.op("vector", lambda e, g=g: e.tensor_tensor(
                out=self.VB[0][:], in0=self.KR[g % 2][:],
                in1=self.KR[g % 2][:], op=ALU.mult))
            sqmk[g] = E.mark("vector", "VE")
            if g + 2 < NG:
                x2_load(g + 2)
            if g >= 1 and g % 2 == 1 and abv_i < len(ab_v_steps):
                ab_v_steps[abv_i]()
                abv_i += 1
        while abv_i < len(ab_v_steps):
            ab_v_steps[abv_i]()
            abv_i += 1
        E.wait("sync", "VE", VM[5])
        E.op("sync", lambda e: e.dma_start(out=self.ab_h[:, :], in_=A[3][:]),
             inc=("LD", 16))
        self.absp = E.count("LD")
        E.wait("sync", "LD", self.absp)
        g_marks = self.stats_dual(
            lambda g, n: self.KR[g % 2][:, n * 512:(n + 1) * 512],
            lambda g, n: self.VB[0][:, n * 512:(n + 1) * 512],
            lambda g: [x2ld[g], ("VE", sqmk[g])], A[5], A[6])
        for g in range(NG):
            assert g_marks[g] == stat_mark(g), (g, g_marks[g], stat_mark(g))
        self.ln2_stats_te = g_marks
        E.wait("vector", "LD", self.absp)
        self.ln2_se = self.ln_finalize(A[5], A[6], A[6], A[2], A[3])

    def cm_mix_make(self, pre, dst_g_fn, first_v_waits=()):
        E = self.E
        A = self.A
        ld = {}
        ve_base = E.count("VE")
        marks = {}

        def load(g):
            if g < 1:
                E.wait("gpsimd", "SE", self.ln2_se)
                E.wait("gpsimd", "VE", ve_base)
            if g >= 1:
                E.wait("gpsimd", "VE", marks[g - 1])
            E.op("gpsimd", lambda e, g=g: e.dma_start(
                out=self.VB[0][:], in_=self.ht(self.x2b_h, g)), inc=("GX", 16))
            ld[g] = E.count("GX")

        def compute(g):
            E.wait("vector", "GX", ld[g])
            if g == 0:
                E.wait("vector", "SE", self.ln2_se)
                for sem, val in first_v_waits:
                    E.wait("vector", sem, val)
            E.op("vector", lambda e, g=g: e.tensor_tensor(
                out=A[0][:], in0=self.VB[0][:], in1=A[5][:], op=ALU.subtract))
            E.op("vector", lambda e, g=g: e.tensor_tensor(
                out=A[1][:], in0=A[0][:], in1=A[6][:], op=ALU.mult))
            E.op("vector", lambda e, g=g: e.tensor_scalar(
                out=A[2][:, 1:T], in0=A[1][:, 0:T - 1],
                scalar1=self.dv(pre + "0", g), scalar2=self.p("ln2_b", g),
                op0=ALU.mult, op1=ALU.add))
            E.op("vector", lambda e, g=g: e.tensor_scalar(
                out=A[2][:, 0:1], in0=self.dv(pre + "B", g),
                scalar1=1.0, scalar2=None, op0=ALU.mult, op1=ALU.bypass))
            E.op("vector", lambda e, g=g: e.scalar_tensor_tensor(
                out=dst_g_fn(g, 0, T), in0=A[1][:, 0:T],
                scalar=self.dv(pre + "1", g), in1=A[2][:, 0:T],
                op0=ALU.mult, op1=ALU.add))
            marks[g] = E.mark("vector", "VE")
        return load, compute, marks

    def phase_cm(self):
        E = self.E
        A = self.A
        ck2ld = {}
        ckmk = {}
        submk = {}
        mulmk = {}

        cwk_pre = [self.wsl_load(self.w_h["cwk"], 0),
                   self.wsl_load(self.w_h["cwk"], 1)]

        def ck_load(g):
            bs = "LB%d" % (g % 2)
            if g < 2:
                E.wait("sync", "TE", self.ln2_stats_te[NG - 1])
            else:
                E.wait("sync", "GE", submk[g - 2])
            E.op("sync", lambda e, g=g: e.dma_start(
                out=self.KR[g % 2][:], in_=self.ht(self.x2b_h, g)),
                inc=(bs, 16))
            ck2ld[g] = (bs, E.count(bs))

        def ck_comp(g):
            if g == 0:
                E.wait("gpsimd", "SE", self.ln2_se)
            E.wait("gpsimd", *ck2ld[g])
            if g >= 2:
                E.wait("gpsimd", "VE", mulmk[g - 2])
            E.op("gpsimd", lambda e, g=g: e.tensor_tensor(
                out=A[g % 2][:], in0=self.KR[g % 2][:], in1=A[5][:],
                op=ALU.subtract))
            submk[g] = E.mark("gpsimd", "GE")
            E.wait("vector", "GE", submk[g])
            if g == 0:
                E.wait("vector", "TE", self.pffn_te + 1)
                E.wait("vector", "SE", self.ln2_se)
            E.op("vector", lambda e, g=g: e.tensor_tensor(
                out=A[2 + g % 2][:], in0=A[g % 2][:], in1=A[6][:],
                op=ALU.mult))
            mulmk[g] = E.mark("vector", "VE")
            E.op("vector", lambda e, g=g: e.tensor_scalar(
                out=A[4][:, 1:T], in0=A[2 + g % 2][:, 0:T - 1],
                scalar1=self.dv("CK0", g), scalar2=self.p("ln2_b", g),
                op0=ALU.mult, op1=ALU.add))
            E.op("vector", lambda e, g=g: e.tensor_scalar(
                out=A[4][:, 0:1], in0=self.dv("CKB", g),
                scalar1=1.0, scalar2=None, op0=ALU.mult, op1=ALU.bypass))
            E.op("vector", lambda e, g=g: e.scalar_tensor_tensor(
                out=self.Xg(g, 0, T), in0=A[2 + g % 2][:, 0:T],
                scalar=self.dv("CK1", g), in1=A[4][:, 0:T],
                op0=ALU.mult, op1=ALU.add))
            ckmk[g] = E.mark("vector", "VE")

        ck_load(0)
        ck_load(1)
        for g in range(NG):
            ck_comp(g)
            if g + 2 < NG:
                ck_load(g + 2)
        cmk_ve = ckmk[NG - 1]
        cr_load, cr_comp, cr_marks = self.cm_mix_make(
            "CR", lambda g, lo, hi: self.Yg(g, lo, hi),
            first_v_waits=(("TE", self.pffn_te),))
        k2sp = {}
        sqvm = {}

        def evac_k2(m, gi, banks):
            self.evac_pre("scalar", gi)
            if m >= 1:
                E.wait("scalar", "VE", sqvm[m - 1])
            eb = A[3]
            for n in range(4):
                E.op("scalar", lambda e, n=n, eb=eb, banks=banks: e.activation(
                    out=eb[:, n * 512:(n + 1) * 512], in_=banks[n][:, :],
                    func=AF.Relu, scale=1.0))
            self.evac_post("scalar")
            sm = E.mark("scalar", "SE")
            E.wait("vector", "SE", sm)
            if m >= 2:
                E.wait("vector", "LD", k2sp[m - 2])
            E.op("vector", lambda e, m=m, eb=eb: e.tensor_tensor(
                out=self.KR[m % 2][:], in0=eb[:], in1=eb[:], op=ALU.mult))
            sqvm[m] = E.mark("vector", "VE")
            E.wait("sync", "VE", sqvm[m])
            E.op("sync", lambda e, m=m: e.dma_start(
                out=self.ht(self.k2b_h, m), in_=self.KR[m % 2][:]), inc=("LD", 16))
            k2sp[m] = E.count("LD")
            E.wait("sync", "LD", k2sp[m])

        def cwk_hook(m):
            cr_load(m)
            cr_comp(m)

        cwk_te = self.weight_phase(
            "cwk", lambda g, n: self.Xg(g, n * 512, (n + 1) * 512),
            evac_k2,
            m0_per_g=(lambda g: [("VE", ckmk[g])]) if "cwk" in PACE else None,
            rhs_waits=() if "cwk" in PACE else (("VE", cmk_ve),),
            hook=cwk_hook, pre_slices=cwk_pre)
        cmr_ve = cr_marks[NG - 1]
        self.cmr_done = cmr_ve
        self.r2ev = {}

        def evac_r2(m, gi, banks):
            self.evac_pre("scalar", gi)
            eb = A[3]
            db = A[4]
            if m == 0:
                E.wait("scalar", "VE", sqvm[NM - 1])
            for n in range(4):
                E.op("scalar", lambda e, n=n, eb=eb, banks=banks: e.activation(
                    out=eb[:, n * 512:(n + 1) * 512], in_=banks[n][:, :],
                    func=AF.Exp, scale=-1.0))
            self.evac_post("scalar")
            sm = E.mark("scalar", "SE")
            E.wait("vector", "SE", sm)
            if m >= 1:
                E.wait("vector", "SE", self.r2ev[m - 1])
            E.op("vector", lambda e, eb=eb, db=db: e.tensor_scalar(
                out=db[:], in0=eb[:], scalar1=1.0, scalar2=None,
                op0=ALU.add, op1=ALU.bypass))
            vm = E.mark("vector", "VE")
            E.wait("scalar", "VE", vm)
            E.op("scalar", lambda e, eb=eb, db=db: e.activation(
                out=eb[:], in_=db[:], func=AF.Ln, scale=1.0))
            E.op("scalar", lambda e, m=m, eb=eb: e.activation(
                out=self.Xg(m), in_=eb[:], func=AF.Exp, scale=-1.0))
            self.r2ev[m] = E.mark("scalar", "SE")

        cwr_te = self.weight_phase(
            "cwr", lambda g, n: self.Yg(g, n * 512, (n + 1) * 512),
            evac_r2,
            m0_per_g=(lambda g: [("VE", cr_marks[g])]) if "cwr" in PACE else None,
            rhs_waits=() if "cwr" in PACE else (("VE", cmr_ve),))
        cwv_pre = [self.wsl_load(self.w_h["cwv"], 0),
                   self.wsl_load(self.w_h["cwv"], 1)]
        E.wait("gpsimd", "TE", cwr_te)
        E.wait("sync", "TE", cwr_te)
        k2ld = {}
        for g in range(NG):
            q = "gpsimd" if g % 2 == 0 else "sync"
            sem = ("GL" if g % 4 == 0 else "GK") if g % 2 == 0 else "LD"
            E.wait(q, "LD", k2sp[g])
            if q == "gpsimd" and g >= 4:
                E.wait(q, sem, k2ld[g - 4][1])
            E.op(q, lambda e, g=g: e.dma_start(
                out=self.Yg(g), in_=self.ht(self.k2b_h, g)), inc=(sem, 16))
            k2ld[g] = (sem, E.count(sem))
            if q == "sync":
                E.wait("sync", "LD", k2ld[g][1])
        k2gw = [("GL", E.count("GL")), ("LD", E.count("LD"))]
        E.wait("sync", "VE", self.cmr_done)
        E.wait("sync", "SE", self.r2ev[NM - 1])
        E.op("sync", lambda e: e.dma_start(out=A[3][:], in_=self.ab_h[:, :]),
             inc=("LD", 16))
        abld = E.count("LD")
        E.wait("sync", "LD", abld)
        ld = {}
        st = {}
        ve_base = E.count("VE")
        fve = lambda m: ve_base + m + 1

        def fin_load(m):
            if m < 2:
                E.wait("sync", "SE", self.r2ev.get(NM - 1, 0) if len(self.r2ev) == NM else 0)
            if m >= 2:
                E.wait("sync", "VE", fve(m - 2))
            E.op("sync", lambda e, m=m: e.dma_start(
                out=A[m % 2][:], in_=self.ht(self.out1_h, m)), inc=("LD", 16))
            E.op("sync", lambda e, m=m: e.dma_start(
                out=self.KR[m % 2][:], in_=self.ht(self.fb_h, m)), inc=("LD", 16))
            ld[m] = E.count("LD")
            E.wait("sync", "LD", ld[m])

        def evac_cm(m, gi, banks):
            self.evac_pre("vector", gi)
            E.wait("vector", "SE", self.r2ev[m])
            E.wait("vector", "LD", ld[m])
            if m == 0:
                E.wait("vector", "LD", abld)
                E.wait("vector", "LD", k2sp[NM - 1])
                E.wait("vector", "SE", self.r2ev[NM - 1])
            if m >= 1:
                E.wait("vector", "LD", st[m - 1])
            for n in range(4):
                E.op("vector", lambda e, n=n, m=m, banks=banks: e.tensor_tensor(
                    out=A[4][:, n * 512:(n + 1) * 512],
                    in0=banks[n][:, :],
                    in1=self.Xg(m, n * 512, (n + 1) * 512), op=ALU.mult))
            self.evac_post("vector")
            E.op("vector", lambda e, m=m: e.tensor_tensor(
                out=A[5][:], in0=self.KR[m % 2][:], in1=A[3][:], op=ALU.mult))
            E.op("vector", lambda e, m=m: e.tensor_tensor(
                out=A[6][:], in0=A[5][:], in1=A[4][:], op=ALU.add))
            E.op("vector", lambda e, m=m: e.tensor_tensor(
                out=A[5][:], in0=A[6][:], in1=A[m % 2][:], op=ALU.add))
            vm = E.mark("vector", "VE")
            assert vm == fve(m)
            E.wait("sync", "VE", vm)
            E.op("sync", lambda e, m=m: e.dma_start(
                out=self.ht(self.out_h, m), in_=A[5][:]), inc=("LD", 16))
            st[m] = E.count("LD")
            if m + 2 < NM:
                fin_load(m + 2)

        fin_load(0)
        fin_load(1)
        self.cm_te = self.weight_phase(
            "cwv", lambda g, n: self.Yg(g, n * 512, (n + 1) * 512),
            evac_cm,
            m0_per_g=(lambda g: [k2ld[g]]) if "cwv" in PACE else None,
            rhs_waits=() if "cwv" in PACE else tuple(k2gw),
            pre_slices=cwv_pre)

    def phase_final(self):
        E = self.E
        for s in ("LD", "LB0", "LB1", "LW", "LV"):
            E.wait("sync", s, E.count(s))
        for s in ("GW", "GK", "GN", "GV", "GX", "GL", "W0", "W1"):
            E.wait("gpsimd", s, E.count(s))

    def build(self, stop_after=None):
        phases = [("init", self.init_consts)] + \
            ([] if SKIP_P0 else [("p0", self.phase0)]) + \
            [("pk", self.phase_k), ("pv", self.phase_v),
             ("pr", self.phase_r), ("att", self.phase_att),
             ("ffn", self.phase_ffn), ("cm", self.phase_cm),
             ("fin", self.phase_final)]
        with ExitStack() as ctx:
            self.alloc(ctx)
            for name, fn in phases:
                fn()
                if stop_after is not None and name == stop_after:
                    break
            if stop_after is not None and stop_after != "fin":
                for s in ("LD", "LB0", "LB1", "LW", "LV"):
                    self.E.wait("sync", s, self.E.count(s))
                for s in ("GW", "GK", "GN", "GV", "GX", "GL", "W0", "W1"):
                    self.E.wait("gpsimd", s, self.E.count(s))
            self.E.replay(self.nc)
        return self.nc


def build_program(dump=False, stop_after=None):
    return KB(dump=dump).build(stop_after=stop_after)


def make_in_maps(inputs):
    x = np.ascontiguousarray(np.asarray(inputs["L0_attn_x"], np.float32))
    vecs = [
        inputs["L0_attn_ln1_w"], inputs["L0_attn_ln1_b"],
        inputs["L0_attn_ln2_w"], inputs["L0_attn_ln2_b"],
        inputs["L0_attn_tm_u"], inputs["L0_attn_tm_w"],
        inputs["L0_attn_tm_mix_k"], inputs["L0_attn_tm_mix_v"],
        inputs["L0_attn_tm_mix_r"], inputs["L0_attn_tm_bo"],
        inputs["L0_attn_cm_mix_k"], inputs["L0_attn_cm_mix_r"],
        inputs["L0_ffn_scale"],
    ]
    pv = np.ascontiguousarray(
        np.stack([np.asarray(v, np.float32).reshape(16, 128).T for v in vecs],
                 axis=1).reshape(128, len(PVN) * 16))
    wT = {
        "wk": np.ascontiguousarray(np.asarray(inputs["L0_attn_tm_Wk"], np.float32).T),
        "wv": np.ascontiguousarray(np.asarray(inputs["L0_attn_tm_Wv"], np.float32).T),
        "wr": np.ascontiguousarray(np.asarray(inputs["L0_attn_tm_Wr"], np.float32).T),
        "wo": np.ascontiguousarray(np.asarray(inputs["L0_attn_tm_Wo"], np.float32).T),
        "cwk": np.ascontiguousarray(np.asarray(inputs["L0_attn_cm_Wk"], np.float32).T),
        "cwv": np.ascontiguousarray(np.asarray(inputs["L0_attn_cm_Wv"], np.float32).T),
        "cwr": np.ascontiguousarray(np.asarray(inputs["L0_attn_cm_Wr"], np.float32).T),
        "wf": np.ascontiguousarray(np.asarray(inputs["L0_ffn_weight"], np.float32)),
    }
    in_maps = []
    for b in range(NCORES):
        m = {"xt": np.ascontiguousarray(x[b].T), "pv": pv}
        m.update(wT)
        in_maps.append(m)
    return in_maps


def kernel(**inputs):
    nc = build_program(dump=False)
    in_maps = make_in_maps(inputs)
    res = run_bass_kernel_spmd(nc, in_maps, core_ids=list(range(NCORES)))
    out = np.stack([np.ascontiguousarray(np.asarray(r["out"]).T)
                    for r in res.results], axis=0)
    return out.astype(np.float32)


# version 20
# speedup vs baseline: 1.0617x; 1.0617x over previous
import numpy as np
import ml_dtypes
import concourse.bass as bass
import concourse.mybir as mybir
from concourse.bass_utils import run_bass_kernel_spmd
from contextlib import ExitStack

F32 = mybir.dt.float32
BF16 = mybir.dt.bfloat16
AF = mybir.ActivationFunctionType
ALU = mybir.AluOpType

D = 2048
T = 2048
NG = 16
NM = 16
NCORES = 8
EPS_LN = 1e-5
EPS_FFN = 1e-7
RD = 1.0 / D
P0_STOP = None
PK_MODE = None
SKIP_P0 = False
PK_NOSPILL = False
NO_EPI = False
EPI_MODE = "full"
DBG_P0 = False
PACE = {"pk", "pv", "att", "ffn", "cwk", "cwr", "cwv"}

ENGINES = ("sync", "gpsimd", "vector", "scalar", "tensor")
PVN = ["ln1_w", "ln1_b", "ln2_w", "ln2_b", "u", "w", "mk", "mv", "mr",
       "bo", "cmk", "cmr", "scale"]
DVN = ["UW", "UPW", "TW2", "OMK", "OMV", "OMR",
       "CK1", "CK0", "CKB", "CR1", "CR0", "CRB",
       "MK1", "MK0", "MKB", "MV1", "MV0", "MVB", "MR1", "MR0", "MRB"]


class Emitter:
    def sem(self, name):
        self.sems.append(name)
        self.counts[name] = 0

    def count(self, name):
        return self.counts[name]

    class _Op:
        __slots__ = ("fn", "incs")

        def __init__(self, fn):
            self.fn = fn
            self.incs = []

        def __call__(self, eng, sems):
            r = self.fn(eng)
            for sem_name, n in self.incs:
                r.then_inc(sems[sem_name], n)

    def __init__(self):
        self.plan = {e: [] for e in ENGINES}
        self.counts = {}
        self.sems = []
        self.dummy_fns = {}

    def op(self, engine, fn, inc=None):
        entry = Emitter._Op(fn)
        if inc is not None:
            sem_name, n = inc
            self.counts[sem_name] += n
            entry.incs.append((sem_name, n))
        self.plan[engine].append(entry)
        return self.counts[inc[0]] if inc is not None else None

    def wait(self, engine, sem_name, val):
        if val is None or val <= 0:
            return

        def wrapped(eng, sems, sem_name=sem_name, val=val):
            eng.wait_ge(sems[sem_name], val)
        self.plan[engine].append(wrapped)

    def mark(self, engine, sem_name):
        assert engine != "tensor"
        self.counts[sem_name] += 1
        last = self.plan[engine][-1] if self.plan[engine] else None
        if isinstance(last, Emitter._Op) and not last.incs:
            last.incs.append((sem_name, 1))
        else:
            entry = Emitter._Op(self.dummy_fns[engine])
            entry.incs.append((sem_name, 1))
            self.plan[engine].append(entry)
        return self.counts[sem_name]

    def replay(self, nc):
        with ExitStack() as ctx:
            sems = {n: ctx.enter_context(nc.semaphore(n)) for n in self.sems}
            with nc.Block() as block:
                for name in ENGINES:
                    ops = self.plan[name]
                    if not ops:
                        continue
                    getattr(block, name)(
                        lambda eng, ops=ops, sems=sems: [f(eng, sems) for f in ops])


class KB:
    def __init__(self, dump=False):
        nc = bass.Bass("TRN2", target_bir_lowering=False,
                       detect_race_conditions=False)
        self.nc = nc
        E = Emitter()
        self.E = E
        for s in ("LD", "GW", "TE", "EV", "VE", "SE", "GE", "DONE",
                  "W0", "W1", "GK", "GN", "GV", "GX", "GL",
                  "LB0", "LB1", "LW", "LV"):
            E.sem(s)
        dt = nc.dram_tensor
        self.xt = dt("xt", [D, T], F32, kind="ExternalInput").ap()
        self.w_h = {}
        for wn in ("wk", "wv", "wr", "wo", "cwk", "cwv", "cwr", "wf"):
            self.w_h[wn] = dt(wn, [D, D], F32, kind="ExternalInput").ap()
        self.pv = dt("pv", [128, len(PVN) * 16], F32, kind="ExternalInput").ap()
        self.out_h = dt("out", [D, T], F32, kind="ExternalOutput").ap()
        ik = "ExternalOutput" if dump else "Internal"
        self.xn32_h = dt("xn32_h", [D, T], F32, kind=ik).ap()
        self.xnb_h = dt("xnb_h", [D, T], BF16, kind=ik).ap()
        self.kb_h = dt("kb_h", [D, T], BF16, kind=ik).ap()
        self.vb_h = dt("vb_h", [D, T], BF16, kind=ik).ap()
        self.x2b_h = dt("x2b_h", [D, T], BF16, kind=ik).ap()
        self.out1_h = dt("out1_h", [D, T], F32, kind=ik).ap()
        self.fb_h = dt("fb_h", [D, T], BF16, kind=ik).ap()
        self.k2b_h = dt("k2b_h", [D, T], BF16, kind=ik).ap()
        self.cmb_h = dt("cmb_h", [D, T], BF16, kind=ik).ap()
        self.wkv_h = dt("wkv_h", [D, T], BF16, kind=ik).ap()
        self.ab_h = dt("ab_h", [128, T], F32, kind=ik).ap()
        if DBG_P0:
            self.dbg1 = dt("dbg1", [128, T], F32, kind="ExternalOutput").ap()
            self.dbg2 = dt("dbg2", [128, T], F32, kind="ExternalOutput").ap()
            self.dbg3 = dt("dbg3", [128, T], F32, kind="ExternalOutput").ap()
        self.gi = 0
        self.te_mark = {}
        self.sl = 0
        self.sl_mark = {}
        self.sl_group = {}

    def alloc(self, ctx):
        nc = self.nc
        sb = lambda n, s, d: ctx.enter_context(nc.sbuf_tensor(n, s, d))
        self.X = sb("X", [128, NG * T], BF16)
        self.Y = sb("Y", [128, NG * T], BF16)
        self.A = [sb(f"A{i}", [128, T], F32) for i in range(7)]
        self.KR = [sb(f"KR{i}", [128, T], BF16) for i in range(2)]
        self.VB = [sb(f"VB{i}", [128, T], BF16) for i in range(1)]
        self.WSL = [sb(f"WSL{i}", [128, NG * 128], BF16) for i in range(2)]
        self.PV = sb("PV", [128, len(PVN) * 16], F32)
        self.DV = sb("DV", [128, len(DVN) * 16], F32)
        self.ONES = sb("ONES", [128, 128], BF16)
        self.EPSC = sb("EPSC", [128, 2], F32)
        self.WUC = sb("WUC", [128, 16], F32)
        self.ps = [ctx.enter_context(nc.psum_tensor(f"ps{i}", [128, 512], F32))
                   for i in range(8)]
        self.E.dummy_fns = {
            "vector": lambda e: e.memset(self.WUC[:, 2:3], 0.0),
            "scalar": lambda e: e.copy(out=self.WUC[:, 3:4], in_=self.EPSC[:, 1:2]),
            "gpsimd": lambda e: e.memset(self.WUC[:, 4:5], 0.0),
        }

    def p(self, name, g):
        i = PVN.index(name)
        return self.PV[:, i * 16 + g: i * 16 + g + 1]

    def dv(self, name, g):
        i = DVN.index(name)
        return self.DV[:, i * 16 + g: i * 16 + g + 1]

    def Xg(self, g, lo=0, hi=T):
        return self.X[:, g * T + lo: g * T + hi]

    def Yg(self, g, lo=0, hi=T):
        return self.Y[:, g * T + lo: g * T + hi]

    def ht(self, h, g):
        return h[g * 128:(g + 1) * 128, :]

    def init_consts(self):
        E = self.E
        E.op("sync", lambda e: e.dma_start(out=self.PV[:], in_=self.pv[:, :]),
             inc=("LD", 16))
        pv_ld = E.count("LD")
        E.op("vector", lambda e: e.memset(self.ONES[:], 1.0))
        E.op("vector", lambda e: e.memset(self.EPSC[:, 0:1], EPS_LN))
        E.op("vector", lambda e: e.memset(self.EPSC[:, 1:2], 1.0))
        E.op("vector", lambda e: e.memset(self.WUC[:], 1.0))
        E.wait("vector", "LD", pv_ld)
        pvi = {n: i for i, n in enumerate(PVN)}
        dvi = {n: i for i, n in enumerate(DVN)}

        def pvs(n):
            i = pvi[n]
            return self.PV[:, i * 16:(i + 1) * 16]

        def dvs(n):
            i = dvi[n]
            return self.DV[:, i * 16:(i + 1) * 16]
        E.op("vector", lambda e: e.tensor_tensor(
            out=dvs("UW"), in0=pvs("u"), in1=pvs("w"), op=ALU.subtract))
        E.op("vector", lambda e: e.tensor_tensor(
            out=dvs("UPW"), in0=pvs("u"), in1=pvs("w"), op=ALU.add))
        E.op("vector", lambda e: e.tensor_scalar(
            out=dvs("TW2"), in0=pvs("w"), scalar1=2.0, scalar2=None,
            op0=ALU.mult, op1=ALU.bypass))
        for dst, src in (("OMK", "mk"), ("OMV", "mv"), ("OMR", "mr")):
            E.op("vector", lambda e, dst=dst, src=src: e.tensor_scalar(
                out=dvs(dst), in0=pvs(src), scalar1=-1.0, scalar2=1.0,
                op0=ALU.mult, op1=ALU.add))
        for pre, mx in (("MK", "mk"), ("MV", "mv"), ("MR", "mr")):
            E.op("vector", lambda e, pre=pre, mx=mx: e.tensor_tensor(
                out=dvs(pre + "1"), in0=pvs(mx), in1=pvs("ln1_w"), op=ALU.mult))
            E.op("vector", lambda e, pre=pre, mx=mx: e.tensor_tensor(
                out=dvs(pre + "B"), in0=pvs(mx), in1=pvs("ln1_b"), op=ALU.mult))
            E.op("vector", lambda e, pre=pre: e.tensor_tensor(
                out=dvs(pre + "0"), in0=pvs("ln1_w"), in1=dvs(pre + "1"),
                op=ALU.subtract))
        for pre, mx in (("CK", "cmk"), ("CR", "cmr")):
            E.op("vector", lambda e, pre=pre, mx=mx: e.tensor_tensor(
                out=dvs(pre + "1"), in0=pvs(mx), in1=pvs("ln2_w"), op=ALU.mult))
            E.op("vector", lambda e, pre=pre, mx=mx: e.tensor_tensor(
                out=dvs(pre + "B"), in0=pvs(mx), in1=pvs("ln2_b"), op=ALU.mult))
            E.op("vector", lambda e, pre=pre: e.tensor_tensor(
                out=dvs(pre + "0"), in0=pvs("ln2_w"), in1=dvs(pre + "1"),
                op=ALU.subtract))
        self.dv_ve = E.mark("vector", "VE")
        E.wait("scalar", "VE", self.dv_ve)
        for i in range(24):
            E.op("scalar", lambda e: e.activation(
                out=self.WUC[:, 0:1], in_=self.EPSC[:, 1:2], func=AF.Exp, scale=0.0))
            E.op("scalar", lambda e: e.activation(
                out=self.WUC[:, 1:2], in_=self.EPSC[:, 1:2], func=AF.Ln, scale=0.0,
                bias=self.EPSC[:, 1:2]))

    def new_group(self):
        gi = self.gi
        self.gi += 1
        banks = [self.ps[(gi % 2) * 4 + n] for n in range(4)]
        return gi, banks

    def mm_group(self, lhsT_fn, rhs_fn, pre_waits=(), per_g_waits=None):
        E = self.E
        gi, banks = self.new_group()
        if gi >= 2:
            E.wait("tensor", "EV", gi - 1)
        for sem, val in pre_waits:
            E.wait("tensor", sem, val)
        for g in range(NG):
            if per_g_waits is not None:
                for sem, val in per_g_waits(g):
                    E.wait("tensor", sem, val)
            for n in range(4):
                last = (g == NG - 1 and n == 3)
                E.op("tensor", lambda e, g=g, n=n, lhsT_fn=lhsT_fn, rhs_fn=rhs_fn, banks=banks: e.matmul(
                    banks[n][:, :], lhsT_fn(g), rhs_fn(g, n),
                    start=(g == 0), stop=(g == NG - 1)),
                    inc=(("TE", 1) if last else None))
        self.te_mark[gi] = E.count("TE")
        return gi, banks

    def evac_pre(self, engine, gi):
        E = self.E
        E.wait(engine, "TE", self.te_mark[gi])
        E.wait(engine, "EV", gi)

    def evac_post(self, engine):
        return self.E.mark(engine, "EV")

    def wsl_load(self, w_ap, m):
        E = self.E
        s = self.sl
        if s >= 2:
            gprev = self.sl_group.get(s - 2)
            if gprev is not None:
                E.wait("gpsimd", "TE", self.te_mark[gprev])
        wr = w_ap.rearrange("(g p) c -> p g c", p=128)
        dst = self.WSL[s % 2][:, :].rearrange("p (g c) -> p g c", c=128)
        wsem = "W%d" % (s % 2)
        E.op("gpsimd", lambda e, wr=wr, dst=dst, m=m: e.dma_start(
            out=dst, in_=wr[:, :, m * 128:(m + 1) * 128]), inc=(wsem, 16))
        self.sl_mark[s] = (wsem, E.count(wsem))
        self.sl += 1
        return s

    def weight_phase(self, wname, rhs_fn, evac_fn, rhs_waits=(), hook=None,
                     m0_per_g=None, pre_slices=None):
        w_ap = self.w_h[wname]
        E = self.E
        slices = (list(pre_slices) if pre_slices is not None else
                  [self.wsl_load(w_ap, 0), self.wsl_load(w_ap, 1)])
        for m in range(NM):
            s = slices[m]
            pw = list(rhs_waits) if m == 0 else []
            pw.append(self.sl_mark[s])
            gi, banks = self.mm_group(
                lambda g, s=s: self.WSL[s % 2][:, g * 128:(g + 1) * 128],
                rhs_fn, pre_waits=pw,
                per_g_waits=(m0_per_g if m == 0 else None))
            self.sl_group[s] = gi
            if m + 2 < NM:
                slices.append(self.wsl_load(w_ap, m + 2))
            evac_fn(m, gi, banks)
            if hook is not None:
                hook(m)
        return self.E.count("TE")

    def stats_dual(self, rhs1_fn, rhs2_fn, ready_fn, mu_buf, ex2_buf):
        E = self.E
        gi = self.gi
        self.gi += 2
        if gi % 2 == 0:
            b1 = [self.ps[n] for n in range(4)]
            b2 = [self.ps[4 + n] for n in range(4)]
        else:
            b1 = [self.ps[4 + n] for n in range(4)]
            b2 = [self.ps[n] for n in range(4)]
        E.wait("tensor", "EV", gi)
        g_marks = {}
        for g in range(NG):
            rd = ready_fn(g)
            if rd is not None:
                if isinstance(rd, list):
                    for sem, val in rd:
                        E.wait("tensor", sem, val)
                else:
                    E.wait("tensor", rd[0], rd[1])
            for n in range(4):
                E.op("tensor", lambda e, g=g, n=n, b1=b1, rhs1_fn=rhs1_fn: e.matmul(
                    b1[n][:, :], self.ONES[:, :], rhs1_fn(g, n),
                    start=(g == 0), stop=(g == NG - 1)))
            for n in range(4):
                last = (n == 3)
                E.op("tensor", lambda e, g=g, n=n, b2=b2, rhs2_fn=rhs2_fn: e.matmul(
                    b2[n][:, :], self.ONES[:, :], rhs2_fn(g, n),
                    start=(g == 0), stop=(g == NG - 1)),
                    inc=(("TE", 1) if last else None))
            g_marks[g] = E.count("TE")
        self.te_mark[gi] = self.te_mark[gi + 1] = E.count("TE")
        self.evac_pre("vector", gi)
        for n in range(4):
            E.op("vector", lambda e, n=n, b1=b1, mu_buf=mu_buf: e.tensor_scalar(
                out=mu_buf[:, n * 512:(n + 1) * 512], in0=b1[n][:, :],
                scalar1=RD, scalar2=None, op0=ALU.mult, op1=ALU.bypass))
        for n in range(4):
            E.op("vector", lambda e, n=n, b2=b2, ex2_buf=ex2_buf: e.tensor_scalar(
                out=ex2_buf[:, n * 512:(n + 1) * 512], in0=b2[n][:, :],
                scalar1=RD, scalar2=None, op0=ALU.mult, op1=ALU.bypass))
        self.evac_post("vector")
        self.evac_post("vector")
        return g_marks

    def ln_finalize(self, mu, ex2, inv_out, t1, t2):
        E = self.E
        E.op("vector", lambda e: e.tensor_tensor(
            out=t1[:], in0=mu[:], in1=mu[:], op=ALU.mult))
        E.op("vector", lambda e: e.tensor_tensor(
            out=t2[:], in0=ex2[:], in1=t1[:], op=ALU.subtract))
        vm = E.mark("vector", "VE")
        E.wait("scalar", "VE", vm)
        E.op("scalar", lambda e: e.activation(
            out=t1[:], in_=t2[:], func=AF.Ln, bias=self.EPSC[:, 0:1], scale=1.0))
        E.op("scalar", lambda e: e.activation(
            out=inv_out[:], in_=t1[:], func=AF.Exp, scale=-0.5))
        return E.mark("scalar", "SE")

    def phase0(self):
        E = self.E
        A = self.A
        self.wk_pre = [self.wsl_load(self.w_h["wk"], 0),
                       self.wsl_load(self.w_h["wk"], 1)]
        ld1 = {}
        te_base = E.count("TE")
        for g in range(NG):
            if g >= 2:
                E.wait("sync", "VE", self.p1ve[g - 2]) if hasattr(self, "p1ve") else None
            if g >= 2:
                E.wait("sync", "TE", te_base + g - 1)
            qs = "LV" if g % 2 == 0 else "GW"
            E.op("sync", lambda e, g=g: e.dma_start(
                out=A[g % 2][:], in_=self.ht(self.xt, g)), inc=(qs, 16))
            ld1[g] = (qs, E.count(qs))
        self.p1ve = {}
        for g in range(NG):
            E.wait("vector", *ld1[g])
            if g >= 1:
                E.wait("vector", "TE", te_base + g)
            E.op("vector", lambda e, g=g: e.tensor_copy(
                out=self.VB[0][:], in_=A[g % 2][:]))
            E.op("vector", lambda e, g=g: e.tensor_tensor(
                out=self.KR[g % 2][:], in0=self.VB[0][:],
                in1=self.VB[0][:], op=ALU.mult))
            self.p1ve[g] = E.mark("vector", "VE")
        if P0_STOP == "pass1pre":
            return
        self.stats_dual(
            lambda g, n: self.VB[0][:, n * 512:(n + 1) * 512],
            lambda g, n: self.KR[g % 2][:, n * 512:(n + 1) * 512],
            lambda g: ("VE", self.p1ve[g]), A[3], A[6])
        if P0_STOP == "stats1":
            return
        se0 = self.ln_finalize(A[3], A[6], A[6], A[2], A[4])
        if P0_STOP == "fin0":
            return
        E.wait("vector", "SE", se0)
        ld2 = {}
        sp2 = {}
        te_base2 = E.count("TE")
        self.p2ve = {}

        def p2_load(g):
            if g < 2:
                E.wait("sync", "VE", self.p1ve[NG - 1])
            qs = "LV" if g % 2 == 0 else "GW"
            E.op("sync", lambda e, g=g: e.dma_start(
                out=A[g % 2][:], in_=self.ht(self.xt, g)), inc=(qs, 16))
            ld2[g] = (qs, E.count(qs))

        p2_load(0)
        p2_load(1)
        for g in range(NG):
            E.wait("vector", *ld2[g])
            if g < 2:
                E.wait("vector", "TE", te_base2)
            if g >= 2:
                E.wait("vector", "LD", sp2[g - 2])
                E.wait("vector", "TE", te_base2 + g - 1)
            E.op("vector", lambda e, g=g: e.tensor_tensor(
                out=A[2][:], in0=A[g % 2][:], in1=A[3][:], op=ALU.subtract))
            E.op("vector", lambda e, g=g: e.tensor_tensor(
                out=A[4 + g % 2][:], in0=A[2][:], in1=A[6][:], op=ALU.mult))
            E.op("vector", lambda e, g=g: e.tensor_copy(
                out=self.Xg(g), in_=A[4 + g % 2][:]))
            E.op("vector", lambda e, g=g: e.tensor_tensor(
                out=self.KR[g % 2][:], in0=self.Xg(g), in1=self.Xg(g), op=ALU.mult))
            self.p2ve[g] = E.mark("vector", "VE")
            E.wait("sync", "VE", self.p2ve[g])
            if g >= 1:
                E.wait("sync", "LD", sp2[g - 1])
            E.op("sync", lambda e, g=g: e.dma_start(
                out=self.ht(self.xn32_h, g), in_=A[4 + g % 2][:]), inc=("LD", 16))
            E.op("sync", lambda e, g=g: e.dma_start(
                out=self.ht(self.xnb_h, g), in_=self.Xg(g)), inc=("LD", 16))
            sp2[g] = E.count("LD")
            if g + 2 < NG:
                p2_load(g + 2)
        self.xnb_sp = sp2
        if P0_STOP == "pass2":
            return
        self.stats_dual(
            lambda g, n: self.Xg(g, n * 512, (n + 1) * 512),
            lambda g, n: self.KR[g % 2][:, n * 512:(n + 1) * 512],
            lambda g: ("VE", self.p2ve[g]), A[3], A[6])
        se1 = self.ln_finalize(A[3], A[6], A[6], A[2], A[4])
        if P0_STOP == "fin1":
            return
        self.se1 = se1
        self.sp2_last = sp2[NG - 1]

    def tm_mix(self, pre, wait_te=None, g_waits=None, first_waits=()):
        E = self.E
        A = self.A
        marks = {}
        for g in range(NG):
            if g == 0 and wait_te is not None:
                E.wait("vector", "TE", wait_te)
            if g == 0:
                for sem, val in first_waits:
                    E.wait("vector", sem, val)
            if g_waits is not None:
                E.wait("vector", *g_waits(g))
            sc = A[2 + g % 2]
            E.op("vector", lambda e, g=g, sc=sc: e.tensor_scalar(
                out=sc[:, 1:T], in0=self.Xg(g, 0, T - 1),
                scalar1=self.dv(pre + "0", g), scalar2=self.p("ln1_b", g),
                op0=ALU.mult, op1=ALU.add))
            E.op("vector", lambda e, g=g, sc=sc: e.tensor_scalar(
                out=sc[:, 0:1], in0=self.dv(pre + "B", g),
                scalar1=1.0, scalar2=None, op0=ALU.mult, op1=ALU.bypass))
            E.op("vector", lambda e, g=g, sc=sc: e.scalar_tensor_tensor(
                out=self.Yg(g), in0=self.Xg(g), scalar=self.dv(pre + "1", g),
                in1=sc[:], op0=ALU.mult, op1=ALU.add))
            marks[g] = E.mark("vector", "VE")
        return marks

    def evac_spill(self, spill_h, track, prior_ld=None):
        E = self.E

        def evac(m, gi, banks):
            self.evac_pre("scalar", gi)
            if m == 0 and prior_ld is not None:
                E.wait("scalar", *prior_ld())
            if m >= 2:
                E.wait("scalar", *track[m - 2])
            for n in range(4):
                E.op("scalar", lambda e, n=n, m=m, banks=banks: e.copy(
                    out=self.KR[m % 2][:, n * 512:(n + 1) * 512],
                    in_=banks[n][:, :]))
            self.evac_post("scalar")
            sm = E.mark("scalar", "SE")
            if PK_NOSPILL:
                track[m] = ("LD", 0)
                return
            E.wait("sync", "SE", sm)
            bsem = "LB%d" % (m % 2)
            E.op("sync", lambda e, m=m: e.dma_start(
                out=self.ht(spill_h, m), in_=self.KR[m % 2][:]), inc=(bsem, 16))
            track[m] = (bsem, E.count(bsem))
        return evac

    def phase_k(self):
        E = self.E
        if PK_MODE == "memsetY":
            for g in range(NG):
                self.E.op("vector", lambda e, g=g: e.memset(self.Yg(g), 0.25))
            self.kmix_ve = E.mark("vector", "VE")
            kmk = {g: self.kmix_ve for g in range(NG)}
        else:
            E.wait("vector", "SE", self.se1)
            E.wait("vector", "LD", self.sp2_last)
            E.wait("gpsimd", "SE", self.se1)
            E.wait("gpsimd", "LD", self.sp2_last)
            self.t2mk = {}
            kmk = {}
            for g in range(NG):
                if g >= 2:
                    E.wait("gpsimd", "VE", self.t2mk[g - 2])
                E.op("gpsimd", lambda e, g=g: e.tensor_tensor(
                    out=self.A[g % 2][:], in0=self.Xg(g), in1=self.A[3][:],
                    op=ALU.subtract))
                gm = E.mark("gpsimd", "GE")
                E.wait("vector", "GE", gm)
                E.op("vector", lambda e, g=g: e.tensor_tensor(
                    out=self.Xg(g), in0=self.A[g % 2][:], in1=self.A[6][:],
                    op=ALU.mult))
                self.t2mk[g] = E.mark("vector", "VE")
                sc = self.A[4 + g % 2]
                E.op("vector", lambda e, g=g, sc=sc: e.tensor_scalar(
                    out=sc[:, 1:T], in0=self.Xg(g, 0, T - 1),
                    scalar1=self.dv("MK0", g), scalar2=self.p("ln1_b", g),
                    op0=ALU.mult, op1=ALU.add))
                E.op("vector", lambda e, g=g, sc=sc: e.tensor_scalar(
                    out=sc[:, 0:1], in0=self.dv("MKB", g),
                    scalar1=1.0, scalar2=None, op0=ALU.mult, op1=ALU.bypass))
                E.op("vector", lambda e, g=g, sc=sc: e.scalar_tensor_tensor(
                    out=self.Yg(g), in0=self.Xg(g), scalar=self.dv("MK1", g),
                    in1=sc[:], op0=ALU.mult, op1=ALU.add))
                kmk[g] = E.mark("vector", "VE")
            self.a1b_ve = self.t2mk[NG - 1]
            self.kmix_ve = kmk[NG - 1]
        if PK_MODE == "mix":
            return
        self.ksp = {}
        self.pk_te = self.weight_phase(
            "wk", lambda g, n: self.Yg(g, n * 512, (n + 1) * 512),
            self.evac_spill(self.kb_h, self.ksp),
            m0_per_g=(lambda g: [("VE", kmk[g])]) if "pk" in PACE else None,
            rhs_waits=() if "pk" in PACE else (("VE", self.kmix_ve),),
            pre_slices=self.wk_pre)

    def phase_v(self):
        E = self.E
        A = self.A
        vmk = self.tm_mix("MV", wait_te=self.pk_te)
        self.vmix_ve = vmk[NG - 1]
        self.vevmk = {}

        def vev(m, gi, banks):
            self.evac_pre("scalar", gi)
            if m < 2:
                E.wait("scalar", *self.ksp[NM - 1])
            else:
                E.wait("scalar", "GE", self.M4[m - 2])
            for n in range(4):
                E.op("scalar", lambda e, n=n, m=m, banks=banks: e.copy(
                    out=self.KR[m % 2][:, n * 512:(n + 1) * 512],
                    in_=banks[n][:, :]))
            self.evac_post("scalar")
            self.vevmk[m] = E.mark("scalar", "SE")
        self.M1 = {}; self.M2 = {}; self.M3 = {}; self.M4 = {}
        self.M5 = {}; self.M6 = {}; self.M7 = {}; self.M8 = {}
        self.kld = {}; self.vld = {}; self.wkvsp = {}

        def S(g):
            b = 4 * (g % 2)
            return A[b], A[b + 1], A[b + 2], A[3]

        def emit_kb_load(g):
            s0, s1, s2, s3 = S(g)
            E.wait("gpsimd", *self.ksp[g])
            if g >= 1:
                E.wait("gpsimd", "GK", self.kld[g - 1])
            if g >= 2:
                E.wait("gpsimd", "SE", self.M6[g - 2])
            E.op("gpsimd", lambda e, g=g, s0=s0: e.dma_start(
                out=s0[:], in_=self.ht(self.kb_h, g)), inc=("GK", 16))
            self.kld[g] = E.count("GK")

        def emit_args(g):
            s0, s1, s2, s3 = S(g)
            E.wait("vector", "GK", self.kld[g])
            if g >= 2:
                E.wait("vector", "GE", self.M7[g - 2])
            E.op("vector", lambda e, g=g, s0=s0, s1=s1: e.tensor_scalar(
                out=s1[:], in0=s0[:], scalar1=self.dv("UW", g), scalar2=0.0,
                op0=ALU.add, op1=ALU.min))
            E.op("vector", lambda e, g=g, s0=s0, s2=s2: e.tensor_scalar(
                out=s2[:], in0=s0[:], scalar1=self.dv("UPW", g),
                scalar2=self.dv("TW2", g), op0=ALU.add, op1=ALU.max))
            self.M1[g] = E.mark("vector", "VE")

        def emit_exps(g):
            s0, s1, s2, s3 = S(g)
            E.wait("scalar", "VE", self.M1[g])
            if g >= 1:
                E.wait("scalar", "VE", self.M5[g - 1])
            E.op("scalar", lambda e, s2=s2, s3=s3: e.activation(
                out=s3[:], in_=s2[:], func=AF.Exp, scale=-1.0))
            E.op("scalar", lambda e, s1=s1, s2=s2: e.activation(
                out=s2[:], in_=s1[:], func=AF.Exp, scale=1.0))
            self.M2[g] = E.mark("scalar", "SE")

        def emit_b(g):
            s0, s1, s2, s3 = S(g)
            E.wait("vector", "SE", self.M2[g])
            E.op("vector", lambda e, s1=s1, s2=s2, s3=s3: e.tensor_tensor_scan(
                out=s1[:], data0=s3[:], data1=s2[:], initial=0.0,
                op0=ALU.mult, op1=ALU.add))
            self.M3[g] = E.mark("vector", "VE")

        def emit_e2v(g):
            s0, s1, s2, s3 = S(g)
            E.wait("gpsimd", "SE", self.M2[g])
            E.wait("gpsimd", "VE", self.M1[g])
            E.wait("gpsimd", "SE", self.vevmk[g])
            E.op("gpsimd", lambda e, s0=s0, s2=s2, g=g: e.tensor_tensor(
                out=s0[:], in0=s2[:], in1=self.KR[g % 2][:], op=ALU.mult))
            self.M4[g] = E.mark("gpsimd", "GE")

        def emit_a(g):
            s0, s1, s2, s3 = S(g)
            E.wait("vector", "GE", self.M4[g])
            E.op("vector", lambda e, s0=s0, s2=s2, s3=s3: e.tensor_tensor_scan(
                out=s2[:], data0=s3[:], data1=s0[:], initial=0.0,
                op0=ALU.mult, op1=ALU.add))
            self.M5[g] = E.mark("vector", "VE")

        def emit_binv(g):
            s0, s1, s2, s3 = S(g)
            E.wait("scalar", "VE", self.M5[g])
            E.op("scalar", lambda e, s0=s0, s1=s1: e.activation(
                out=s0[:], in_=s1[:], func=AF.Ln, scale=1.0))
            E.op("scalar", lambda e, s0=s0, s1=s1: e.activation(
                out=s1[:], in_=s0[:], func=AF.Exp, scale=-1.0))
            self.M6[g] = E.mark("scalar", "SE")

        def emit_wkv(g):
            s0, s1, s2, s3 = S(g)
            E.wait("gpsimd", "VE", self.M5[g])
            E.wait("gpsimd", "SE", self.M6[g])
            if g >= 1:
                E.wait("gpsimd", "LW", self.wkvsp[g - 1])
            E.op("gpsimd", lambda e, s1=s1, s2=s2, g=g: e.tensor_tensor(
                out=self.VB[0][:], in0=s2[:], in1=s1[:], op=ALU.mult))
            self.M7[g] = E.mark("gpsimd", "GE")
            E.wait("sync", "GE", self.M7[g])
            E.op("sync", lambda e, g=g: e.dma_start(
                out=self.ht(self.wkv_h, g), in_=self.VB[0][:]), inc=("LW", 16))
            self.wkvsp[g] = E.count("LW")

        def hook(m):
            if m >= 2:
                emit_wkv(m - 2)
            emit_kb_load(m)
            if m >= 1:
                emit_b(m - 1)
                emit_e2v(m - 1)
                emit_a(m - 1)
                emit_binv(m - 1)
            emit_args(m)
            emit_exps(m)

        self.pv_te = self.weight_phase(
            "wv", lambda g, n: self.Yg(g, n * 512, (n + 1) * 512),
            vev, m0_per_g=(lambda g: [("VE", vmk[g])]) if "pv" in PACE else None,
            rhs_waits=() if "pv" in PACE else (("VE", self.vmix_ve),), hook=hook)
        emit_wkv(NG - 2)
        emit_b(NG - 1)
        emit_e2v(NG - 1)
        emit_a(NG - 1)
        emit_binv(NG - 1)
        emit_wkv(NG - 1)

    def mix_sigmoid(self, src_g_fn, dst_g_fn, m_name, om_name, first_waits=()):
        E = self.E
        A = self.A
        VH = {}
        S1 = {}
        VD = {}
        SF = {}

        def v_head(g):
            sb_ = A[g % 3]
            mb_ = A[3 + g % 3]
            if g >= 3:
                E.wait("vector", "SE", SF[g - 3])
            E.op("vector", lambda e, g=g, sb_=sb_: e.tensor_scalar(
                out=sb_[:, 1:T], in0=src_g_fn(g, 0, T - 1),
                scalar1=self.dv(m_name + "0", g), scalar2=self.p("ln1_b", g),
                op0=ALU.mult, op1=ALU.add))
            E.op("vector", lambda e, g=g, sb_=sb_: e.tensor_scalar(
                out=sb_[:, 0:1], in0=self.dv(m_name + "B", g),
                scalar1=1.0, scalar2=None, op0=ALU.mult, op1=ALU.bypass))
            E.op("vector", lambda e, g=g, sb_=sb_, mb_=mb_: e.scalar_tensor_tensor(
                out=mb_[:], in0=src_g_fn(g, 0, T), scalar=self.dv(m_name + "1", g),
                in1=sb_[:], op0=ALU.mult, op1=ALU.add))
            VH[g] = E.mark("vector", "VE")

        def s_e(g):
            sb_ = A[g % 3]
            mb_ = A[3 + g % 3]
            E.wait("scalar", "VE", VH[g])
            E.op("scalar", lambda e, sb_=sb_, mb_=mb_: e.activation(
                out=sb_[:], in_=mb_[:], func=AF.Exp, scale=-1.0))
            S1[g] = E.mark("scalar", "SE")

        def v_d(g):
            sb_ = A[g % 3]
            mb_ = A[3 + g % 3]
            E.wait("vector", "SE", S1[g])
            E.op("vector", lambda e, sb_=sb_, mb_=mb_: e.tensor_scalar(
                out=mb_[:], in0=sb_[:], scalar1=1.0, scalar2=None,
                op0=ALU.add, op1=ALU.bypass))
            VD[g] = E.mark("vector", "VE")

        def s_fin(g):
            sb_ = A[g % 3]
            mb_ = A[3 + g % 3]
            E.wait("scalar", "VE", VD[g])
            E.op("scalar", lambda e, sb_=sb_, mb_=mb_: e.activation(
                out=sb_[:], in_=mb_[:], func=AF.Ln, scale=1.0))
            E.op("scalar", lambda e, g=g, sb_=sb_: e.activation(
                out=dst_g_fn(g), in_=sb_[:], func=AF.Exp, scale=-1.0))
            SF[g] = E.mark("scalar", "SE")

        for eng, sem, val in first_waits:
            E.wait(eng, sem, val)
        for g in range(NG):
            v_head(g)
            s_e(g)
            if g >= 1:
                v_d(g - 1)
            if g >= 2:
                s_fin(g - 2)
        v_d(NG - 1)
        s_fin(NG - 2)
        s_fin(NG - 1)
        return SF

    def rmix_sigmoid(self):
        fw = (("vector", "TE", self.pv_te),
              ("vector", "GE", self.M7[NG - 1]),
              ("scalar", "GE", self.M7[NG - 1]))
        return self.mix_sigmoid(
            lambda g, lo, hi: self.Xg(g, lo, hi),
            lambda g: self.Yg(g), "MR", "OMR", first_waits=fw)

    def phase_r(self):
        E = self.E
        A = self.A
        rmix_sf = self.rmix_sigmoid()
        self.rev = {}
        wkld = {}

        def evac_r(m, gi, banks):
            if m == 0:
                E.wait("gpsimd", "LW", self.wkvsp[NM - 1])
            else:
                E.wait("gpsimd", "LW", self.wkvsp[m])
                E.wait("gpsimd", "VE", self.M8[m - 1])
            E.op("gpsimd", lambda e, m=m: e.dma_start(
                out=self.VB[0][:], in_=self.ht(self.wkv_h, m)), inc=("GV", 16))
            wkld[m] = E.count("GV")
            self.evac_pre("scalar", gi)
            if m == 0:
                E.wait("scalar", "GE", self.M4[NM - 1])
            if m >= 2:
                E.wait("scalar", "VE", self.M8[m - 2])
            for n in range(4):
                E.op("scalar", lambda e, n=n, m=m, banks=banks: e.copy(
                    out=self.KR[m % 2][:, n * 512:(n + 1) * 512],
                    in_=banks[n][:, :]))
            self.evac_post("scalar")
            self.rev[m] = E.mark("scalar", "SE")
            E.wait("vector", "GV", wkld[m])
            E.wait("vector", "SE", self.rev[m])
            E.op("vector", lambda e, m=m: e.tensor_tensor(
                out=self.Xg(m), in0=self.KR[m % 2][:], in1=self.VB[0][:],
                op=ALU.mult))
            self.M8[m] = E.mark("vector", "VE")

        self.pr_te = self.weight_phase(
            "wr", lambda g, n: self.Yg(g, n * 512, (n + 1) * 512),
            evac_r, m0_per_g=lambda g: [("SE", rmix_sf[g])])
        self.wkvr_done = self.M8[NG - 1]

    def phase_att(self):
        E = self.E
        A = self.A
        self.xnb_gw = {}

        def xnb_load(m):
            if m == 0:
                E.wait("gpsimd", "TE", self.pr_te)
            else:
                E.wait("gpsimd", "GN", self.xnb_gw[m - 1])
            E.wait("gpsimd", "LD", self.xnb_sp[m])
            E.op("gpsimd", lambda e, m=m: e.dma_start(
                out=self.Yg(m), in_=self.ht(self.xnb_h, m)), inc=("GN", 16))
            self.xnb_gw[m] = E.count("GN")

        ld = {}
        ve_base = E.count("VE")
        self.att_ve_pre = lambda m: ve_base + 2 * (m + 1)

        def att_load(m):
            if m == 0:
                E.wait("sync", "SE", self.rev[NM - 1])
                E.wait("sync", "VE", self.wkvr_done)
            if m >= 2:
                E.wait("sync", "VE", self.att_ve_pre(m - 2))
            E.op("sync", lambda e, m=m: e.dma_start(
                out=A[m % 2][:], in_=self.ht(self.xn32_h, m)), inc=("LD", 16))
            E.op("sync", lambda e, m=m: e.dma_start(
                out=A[2 + m % 2][:], in_=self.ht(self.xt, m)), inc=("LD", 16))
            ld[m] = E.count("LD")
            E.wait("sync", "LD", ld[m])

        att_load(0)
        att_load(1)
        sp = {}
        x2cp = {}

        def evac_att(m, gi, banks):
            self.evac_pre("vector", gi)
            E.wait("vector", "LD", ld[m])
            if m >= 2:
                E.wait("vector", "LD", sp[m - 2])
            if m >= 1:
                E.wait("vector", "SE", x2cp[m - 1])
            t2 = A[4]
            for n in range(4):
                E.op("vector", lambda e, n=n, m=m, banks=banks, t2=t2: e.scalar_tensor_tensor(
                    out=t2[:, n * 512:(n + 1) * 512], in0=banks[n][:, :],
                    scalar=self.p("bo", m), in1=A[m % 2][:, n * 512:(n + 1) * 512],
                    op0=ALU.add, op1=ALU.add))
            self.evac_post("vector")
            ob = A[5 + m % 2]
            E.op("vector", lambda e, m=m, t2=t2, ob=ob: e.tensor_tensor(
                out=ob[:], in0=t2[:], in1=A[2 + m % 2][:], op=ALU.add))
            vm1 = E.mark("vector", "VE")
            E.wait("scalar", "VE", vm1)
            if m >= 2:
                E.wait("scalar", "LD", sp[m - 2])
            E.op("scalar", lambda e, m=m, t2=t2: e.activation(
                out=self.KR[m % 2][:], in_=t2[:], func=AF.Copy, scale=1.0))
            sm = E.mark("scalar", "SE")
            x2cp[m] = sm
            vm2 = E.mark("vector", "VE")
            assert vm2 == self.att_ve_pre(m), (vm2, self.att_ve_pre(m))
            E.wait("sync", "SE", sm)
            E.op("sync", lambda e, m=m: e.dma_start(
                out=self.ht(self.x2b_h, m), in_=self.KR[m % 2][:]), inc=("LD", 16))
            E.wait("sync", "VE", vm1)
            E.op("sync", lambda e, m=m, ob=ob: e.dma_start(
                out=self.ht(self.out1_h, m), in_=ob[:]), inc=("LD", 16))
            sp[m] = E.count("LD")
            if m + 2 < NM:
                att_load(m + 2)
            else:
                E.wait("sync", "LD", sp[m])

        self.wf_pre = None

        def att_hook(m):
            xnb_load(m)
            if m == NM - 1:
                self.wf_pre = [self.wsl_load(self.w_h["wf"], 0),
                               self.wsl_load(self.w_h["wf"], 1)]

        self.patt_te = self.weight_phase(
            "wo", lambda g, n: self.Xg(g, n * 512, (n + 1) * 512),
            evac_att,
            m0_per_g=(lambda g: [("VE", self.M8[g])]) if "att" in PACE else None,
            rhs_waits=() if "att" in PACE else (("VE", self.wkvr_done),),
            hook=att_hook)
        self.ffn_rhs_gw = self.xnb_gw[NM - 1]
        self.x2sp = sp

    def phase_ffn(self):
        E = self.E
        A = self.A
        self.fsp = {}
        self.f2ve = {}

        def evac_f(m, gi, banks):
            self.evac_pre("scalar", gi)
            if m == 0:
                E.wait("scalar", "LD", self.x2sp[NM - 1])
            if m >= 2:
                E.wait("scalar", "LD", self.fsp[m - 2])
            for n in range(4):
                E.op("scalar", lambda e, n=n, m=m, banks=banks: e.activation(
                    out=self.KR[m % 2][:, n * 512:(n + 1) * 512],
                    in_=banks[n][:, :], func=AF.Copy, scale=self.p("scale", m)))
            self.evac_post("scalar")
            sm = E.mark("scalar", "SE")
            E.wait("sync", "SE", sm)
            E.op("sync", lambda e, m=m: e.dma_start(
                out=self.ht(self.fb_h, m), in_=self.KR[m % 2][:]), inc=("LD", 16))
            self.fsp[m] = E.count("LD")
            E.wait("sync", "LD", self.fsp[m])
            E.wait("vector", "SE", sm)
            if m == 0:
                E.wait("vector", "TE", self.patt_te)
            E.op("vector", lambda e, m=m: e.tensor_tensor(
                out=self.Xg(m), in0=self.KR[m % 2][:], in1=self.KR[m % 2][:],
                op=ALU.mult))
            self.f2ve[m] = E.mark("vector", "VE")

        self.pffn_te = self.weight_phase(
            "wf", lambda g, n: self.Yg(g, n * 512, (n + 1) * 512),
            evac_f,
            m0_per_g=(lambda g: [("GN", self.xnb_gw[g])]) if "ffn" in PACE else None,
            rhs_waits=() if "ffn" in PACE else (("GN", self.ffn_rhs_gw),),
            pre_slices=self.wf_pre)

        gi, banks = self.new_group()
        E.wait("tensor", "EV", gi - 1)
        for m in range(NM):
            E.wait("tensor", "VE", self.f2ve[m])
            for n in range(4):
                last = (m == NM - 1 and n == 3)
                E.op("tensor", lambda e, m=m, n=n, banks=banks: e.matmul(
                    banks[n][:, :], self.ONES[:, :],
                    self.Xg(m, n * 512, (n + 1) * 512),
                    start=(m == 0), stop=(m == NM - 1)),
                    inc=(("TE", 1) if last else None))
        self.te_mark[gi] = E.count("TE")
        self.evac_pre("vector", gi)
        for n in range(4):
            E.op("vector", lambda e, n=n, banks=banks: e.tensor_scalar(
                out=A[0][:, n * 512:(n + 1) * 512], in0=banks[n][:, :],
                scalar1=1.0, scalar2=None, op0=ALU.mult, op1=ALU.bypass))
        self.evac_post("vector")
        vm0 = E.mark("vector", "VE")
        E.wait("scalar", "VE", vm0)
        E.op("scalar", lambda e: e.activation(out=A[1][:], in_=A[0][:], func=AF.Ln, scale=1.0))
        E.op("scalar", lambda e: e.activation(out=A[0][:], in_=A[1][:], func=AF.Exp, scale=0.5))
        sm1 = E.mark("scalar", "SE")
        VM = {}

        def sch(k, vops, sops):
            def run(k=k, vops=vops, sops=sops):
                for f in vops:
                    f()
                VM[k] = E.mark("vector", "VE")
                if sops is not None:
                    E.wait("scalar", "VE", VM[k])
                    for f in sops:
                        f()
                    VM[(k, "s")] = E.mark("scalar", "SE")
            return run

        ab_v_steps = [
            sch(1, [lambda: E.wait("vector", "SE", sm1),
                    lambda: E.op("vector", lambda e: e.tensor_scalar(
                        out=A[1][:], in0=A[0][:], scalar1=EPS_FFN, scalar2=EPS_FFN,
                        op0=ALU.max, op1=ALU.add))],
                [lambda: E.op("scalar", lambda e: e.activation(out=A[2][:], in_=A[1][:], func=AF.Ln, scale=1.0)),
                 lambda: E.op("scalar", lambda e: e.activation(out=A[1][:], in_=A[2][:], func=AF.Exp, scale=-1.0))]),
            sch(2, [lambda: E.wait("vector", "SE", VM[(1, "s")]),
                    lambda: E.op("vector", lambda e: e.tensor_scalar(
                        out=A[2][:], in0=A[0][:], scalar1=20.0, scalar2=2.0,
                        op0=ALU.min, op1=ALU.mult))],
                [lambda: E.op("scalar", lambda e: e.activation(out=A[3][:], in_=A[2][:], func=AF.Exp, scale=1.0))]),
            sch(3, [lambda: E.wait("vector", "SE", VM[(2, "s")]),
                    lambda: E.op("vector", lambda e: e.tensor_scalar(
                        out=A[2][:], in0=A[3][:], scalar1=1.0, scalar2=None,
                        op0=ALU.add, op1=ALU.bypass))],
                [lambda: E.op("scalar", lambda e: e.activation(out=A[3][:], in_=A[2][:], func=AF.Ln, scale=1.0)),
                 lambda: E.op("scalar", lambda e: e.activation(out=A[2][:], in_=A[3][:], func=AF.Exp, scale=-1.0))]),
            sch(4, [lambda: E.wait("vector", "SE", VM[(3, "s")]),
                    lambda: E.op("vector", lambda e: e.tensor_scalar(
                        out=A[3][:], in0=A[2][:], scalar1=-2.0, scalar2=1.0,
                        op0=ALU.mult, op1=ALU.add)),
                    lambda: E.op("vector", lambda e: e.tensor_tensor(
                        out=A[2][:], in0=A[3][:], in1=A[1][:], op=ALU.mult)),
                    lambda: E.op("vector", lambda e: e.tensor_tensor(
                        out=A[1][:], in0=A[2][:], in1=A[0][:], op=ALU.mult)),
                    lambda: E.op("vector", lambda e: e.tensor_scalar(
                        out=A[0][:], in0=A[1][:], scalar1=EPS_FFN, scalar2=None,
                        op0=ALU.max, op1=ALU.bypass))],
                [lambda: E.op("scalar", lambda e: e.activation(out=A[1][:], in_=A[0][:], func=AF.Ln, scale=1.0)),
                 lambda: E.op("scalar", lambda e: e.activation(out=A[0][:], in_=A[1][:], func=AF.Exp, scale=-1.0))]),
            sch(5, [lambda: E.wait("vector", "SE", VM[(4, "s")]),
                    lambda: E.op("vector", lambda e: e.tensor_scalar(
                        out=A[1][:], in0=A[0][:], scalar1=0.99, scalar2=1.0,
                        op0=ALU.mult, op1=ALU.min)),
                    lambda: E.op("vector", lambda e: e.tensor_tensor(
                        out=A[3][:], in0=A[2][:], in1=A[1][:], op=ALU.mult))],
                None),
        ]

        x2ld = {}
        sqmk = {}
        tes_base = E.count("TE")
        stat_mark = lambda g: tes_base + g + 1

        def x2_load(g):
            bs = "LB%d" % (g % 2)
            if g >= 2:
                E.wait("sync", "VE", sqmk[g - 2])
                E.wait("sync", "TE", stat_mark(g - 2))
            E.op("sync", lambda e, g=g: e.dma_start(
                out=self.KR[g % 2][:], in_=self.ht(self.x2b_h, g)),
                inc=(bs, 16))
            x2ld[g] = (bs, E.count(bs))

        x2_load(0)
        x2_load(1)
        abv_i = 0
        for g in range(NG):
            E.wait("vector", *x2ld[g])
            if g == 0:
                E.wait("vector", "VE", self.wkvr_done)
            if g >= 1:
                E.wait("vector", "TE", stat_mark(g - 1))
            E.op("vector", lambda e, g=g: e.tensor_tensor(
                out=self.VB[0][:], in0=self.KR[g % 2][:],
                in1=self.KR[g % 2][:], op=ALU.mult))
            sqmk[g] = E.mark("vector", "VE")
            if g + 2 < NG:
                x2_load(g + 2)
            if g >= 1 and g % 2 == 1 and abv_i < len(ab_v_steps):
                ab_v_steps[abv_i]()
                abv_i += 1
        while abv_i < len(ab_v_steps):
            ab_v_steps[abv_i]()
            abv_i += 1
        E.wait("sync", "VE", VM[5])
        E.op("sync", lambda e: e.dma_start(out=self.ab_h[:, :], in_=A[3][:]),
             inc=("LD", 16))
        self.absp = E.count("LD")
        E.wait("sync", "LD", self.absp)
        g_marks = self.stats_dual(
            lambda g, n: self.KR[g % 2][:, n * 512:(n + 1) * 512],
            lambda g, n: self.VB[0][:, n * 512:(n + 1) * 512],
            lambda g: [x2ld[g], ("VE", sqmk[g])], A[5], A[6])
        for g in range(NG):
            assert g_marks[g] == stat_mark(g), (g, g_marks[g], stat_mark(g))
        self.ln2_stats_te = g_marks
        E.wait("vector", "LD", self.absp)
        self.ln2_se = self.ln_finalize(A[5], A[6], A[6], A[2], A[3])

    def cm_mix_make(self, pre, dst_g_fn, first_v_waits=()):
        E = self.E
        A = self.A
        ld = {}
        ve_base = E.count("VE")
        marks = {}

        def load(g):
            if g < 1:
                E.wait("gpsimd", "SE", self.ln2_se)
                E.wait("gpsimd", "VE", ve_base)
            if g >= 1:
                E.wait("gpsimd", "VE", marks[g - 1])
            E.op("gpsimd", lambda e, g=g: e.dma_start(
                out=self.VB[0][:], in_=self.ht(self.x2b_h, g)), inc=("GX", 16))
            ld[g] = E.count("GX")

        def compute(g):
            E.wait("vector", "GX", ld[g])
            if g == 0:
                E.wait("vector", "SE", self.ln2_se)
                for sem, val in first_v_waits:
                    E.wait("vector", sem, val)
            E.op("vector", lambda e, g=g: e.tensor_tensor(
                out=A[0][:], in0=self.VB[0][:], in1=A[5][:], op=ALU.subtract))
            E.op("vector", lambda e, g=g: e.tensor_tensor(
                out=A[1][:], in0=A[0][:], in1=A[6][:], op=ALU.mult))
            E.op("vector", lambda e, g=g: e.tensor_scalar(
                out=A[2][:, 1:T], in0=A[1][:, 0:T - 1],
                scalar1=self.dv(pre + "0", g), scalar2=self.p("ln2_b", g),
                op0=ALU.mult, op1=ALU.add))
            E.op("vector", lambda e, g=g: e.tensor_scalar(
                out=A[2][:, 0:1], in0=self.dv(pre + "B", g),
                scalar1=1.0, scalar2=None, op0=ALU.mult, op1=ALU.bypass))
            E.op("vector", lambda e, g=g: e.scalar_tensor_tensor(
                out=dst_g_fn(g, 0, T), in0=A[1][:, 0:T],
                scalar=self.dv(pre + "1", g), in1=A[2][:, 0:T],
                op0=ALU.mult, op1=ALU.add))
            marks[g] = E.mark("vector", "VE")
        return load, compute, marks

    def phase_cm(self):
        E = self.E
        A = self.A
        ck2ld = {}
        ckmk = {}
        submk = {}
        mulmk = {}

        cwk_pre = [self.wsl_load(self.w_h["cwk"], 0),
                   self.wsl_load(self.w_h["cwk"], 1)]

        def ck_load(g):
            bs = "LB%d" % (g % 2)
            if g < 2:
                E.wait("sync", "TE", self.ln2_stats_te[NG - 1])
            else:
                E.wait("sync", "GE", submk[g - 2])
            E.op("sync", lambda e, g=g: e.dma_start(
                out=self.KR[g % 2][:], in_=self.ht(self.x2b_h, g)),
                inc=(bs, 16))
            ck2ld[g] = (bs, E.count(bs))

        def ck_comp(g):
            if g == 0:
                E.wait("gpsimd", "SE", self.ln2_se)
            E.wait("gpsimd", *ck2ld[g])
            if g >= 2:
                E.wait("gpsimd", "VE", mulmk[g - 2])
            E.op("gpsimd", lambda e, g=g: e.tensor_tensor(
                out=A[g % 2][:], in0=self.KR[g % 2][:], in1=A[5][:],
                op=ALU.subtract))
            submk[g] = E.mark("gpsimd", "GE")
            E.wait("vector", "GE", submk[g])
            if g == 0:
                E.wait("vector", "TE", self.pffn_te + 1)
                E.wait("vector", "SE", self.ln2_se)
            E.op("vector", lambda e, g=g: e.tensor_tensor(
                out=A[2 + g % 2][:], in0=A[g % 2][:], in1=A[6][:],
                op=ALU.mult))
            mulmk[g] = E.mark("vector", "VE")
            E.op("vector", lambda e, g=g: e.tensor_scalar(
                out=A[4][:, 1:T], in0=A[2 + g % 2][:, 0:T - 1],
                scalar1=self.dv("CK0", g), scalar2=self.p("ln2_b", g),
                op0=ALU.mult, op1=ALU.add))
            E.op("vector", lambda e, g=g: e.tensor_scalar(
                out=A[4][:, 0:1], in0=self.dv("CKB", g),
                scalar1=1.0, scalar2=None, op0=ALU.mult, op1=ALU.bypass))
            E.op("vector", lambda e, g=g: e.scalar_tensor_tensor(
                out=self.Xg(g, 0, T), in0=A[2 + g % 2][:, 0:T],
                scalar=self.dv("CK1", g), in1=A[4][:, 0:T],
                op0=ALU.mult, op1=ALU.add))
            ckmk[g] = E.mark("vector", "VE")

        ck_load(0)
        ck_load(1)
        for g in range(NG):
            ck_comp(g)
            if g + 2 < NG:
                ck_load(g + 2)
        cmk_ve = ckmk[NG - 1]
        cr_load, cr_comp, cr_marks = self.cm_mix_make(
            "CR", lambda g, lo, hi: self.Yg(g, lo, hi),
            first_v_waits=(("TE", self.pffn_te),))
        k2sp = {}
        sqvm = {}

        def evac_k2(m, gi, banks):
            self.evac_pre("scalar", gi)
            if m >= 1:
                E.wait("scalar", "VE", sqvm[m - 1])
            eb = A[3]
            for n in range(4):
                E.op("scalar", lambda e, n=n, eb=eb, banks=banks: e.activation(
                    out=eb[:, n * 512:(n + 1) * 512], in_=banks[n][:, :],
                    func=AF.Relu, scale=1.0))
            self.evac_post("scalar")
            sm = E.mark("scalar", "SE")
            E.wait("vector", "SE", sm)
            if m >= 2:
                E.wait("vector", "LD", k2sp[m - 2])
            E.op("vector", lambda e, m=m, eb=eb: e.tensor_tensor(
                out=self.KR[m % 2][:], in0=eb[:], in1=eb[:], op=ALU.mult))
            sqvm[m] = E.mark("vector", "VE")
            E.wait("sync", "VE", sqvm[m])
            E.op("sync", lambda e, m=m: e.dma_start(
                out=self.ht(self.k2b_h, m), in_=self.KR[m % 2][:]), inc=("LD", 16))
            k2sp[m] = E.count("LD")
            E.wait("sync", "LD", k2sp[m])

        def cwk_hook(m):
            cr_load(m)
            cr_comp(m)

        cwk_te = self.weight_phase(
            "cwk", lambda g, n: self.Xg(g, n * 512, (n + 1) * 512),
            evac_k2,
            m0_per_g=(lambda g: [("VE", ckmk[g])]) if "cwk" in PACE else None,
            rhs_waits=() if "cwk" in PACE else (("VE", cmk_ve),),
            hook=cwk_hook, pre_slices=cwk_pre)
        cmr_ve = cr_marks[NG - 1]
        self.cmr_done = cmr_ve
        self.r2ev = {}

        def evac_r2(m, gi, banks):
            self.evac_pre("scalar", gi)
            eb = A[3]
            db = A[4]
            if m == 0:
                E.wait("scalar", "VE", sqvm[NM - 1])
            for n in range(4):
                E.op("scalar", lambda e, n=n, eb=eb, banks=banks: e.activation(
                    out=eb[:, n * 512:(n + 1) * 512], in_=banks[n][:, :],
                    func=AF.Exp, scale=-1.0))
            self.evac_post("scalar")
            sm = E.mark("scalar", "SE")
            E.wait("vector", "SE", sm)
            if m >= 1:
                E.wait("vector", "SE", self.r2ev[m - 1])
            E.op("vector", lambda e, eb=eb, db=db: e.tensor_scalar(
                out=db[:], in0=eb[:], scalar1=1.0, scalar2=None,
                op0=ALU.add, op1=ALU.bypass))
            vm = E.mark("vector", "VE")
            E.wait("scalar", "VE", vm)
            E.op("scalar", lambda e, eb=eb, db=db: e.activation(
                out=eb[:], in_=db[:], func=AF.Ln, scale=1.0))
            E.op("scalar", lambda e, m=m, eb=eb: e.activation(
                out=self.Xg(m), in_=eb[:], func=AF.Exp, scale=-1.0))
            self.r2ev[m] = E.mark("scalar", "SE")

        cwr_te = self.weight_phase(
            "cwr", lambda g, n: self.Yg(g, n * 512, (n + 1) * 512),
            evac_r2,
            m0_per_g=(lambda g: [("VE", cr_marks[g])]) if "cwr" in PACE else None,
            rhs_waits=() if "cwr" in PACE else (("VE", cmr_ve),))
        cwv_pre = [self.wsl_load(self.w_h["cwv"], 0),
                   self.wsl_load(self.w_h["cwv"], 1)]
        E.wait("gpsimd", "TE", cwr_te)
        E.wait("sync", "TE", cwr_te)
        k2ld = {}
        for g in range(NG):
            q = "gpsimd" if g % 2 == 0 else "sync"
            sem = ("GL" if g % 4 == 0 else "GK") if g % 2 == 0 else "LD"
            E.wait(q, "LD", k2sp[g])
            if q == "gpsimd" and g >= 4:
                E.wait(q, sem, k2ld[g - 4][1])
            E.op(q, lambda e, g=g: e.dma_start(
                out=self.Yg(g), in_=self.ht(self.k2b_h, g)), inc=(sem, 16))
            k2ld[g] = (sem, E.count(sem))
            if q == "sync":
                E.wait("sync", "LD", k2ld[g][1])
        k2gw = [("GL", E.count("GL")), ("LD", E.count("LD"))]
        E.wait("sync", "VE", self.cmr_done)
        E.wait("sync", "SE", self.r2ev[NM - 1])
        E.op("sync", lambda e: e.dma_start(out=A[3][:], in_=self.ab_h[:, :]),
             inc=("LD", 16))
        abld = E.count("LD")
        E.wait("sync", "LD", abld)
        ld = {}
        st = {}
        ve_base = E.count("VE")
        fve = lambda m: ve_base + m + 1

        def fin_load(m):
            if m < 2:
                E.wait("sync", "SE", self.r2ev.get(NM - 1, 0) if len(self.r2ev) == NM else 0)
            if m >= 2:
                E.wait("sync", "VE", fve(m - 2))
            qs = "LV" if m % 2 == 0 else "GW"
            E.op("sync", lambda e, m=m: e.dma_start(
                out=A[m % 2][:], in_=self.ht(self.out1_h, m)), inc=(qs, 16))
            E.op("sync", lambda e, m=m: e.dma_start(
                out=self.KR[m % 2][:], in_=self.ht(self.fb_h, m)), inc=(qs, 16))
            ld[m] = (qs, E.count(qs))

        def evac_cm(m, gi, banks):
            self.evac_pre("vector", gi)
            E.wait("vector", "SE", self.r2ev[m])
            E.wait("vector", *ld[m])
            if m == 0:
                E.wait("vector", "LD", abld)
                E.wait("vector", "LD", k2sp[NM - 1])
                E.wait("vector", "SE", self.r2ev[NM - 1])
            if m >= 1:
                E.wait("vector", "LD", st[m - 1])
            for n in range(4):
                E.op("vector", lambda e, n=n, m=m, banks=banks: e.tensor_tensor(
                    out=A[4][:, n * 512:(n + 1) * 512],
                    in0=banks[n][:, :],
                    in1=self.Xg(m, n * 512, (n + 1) * 512), op=ALU.mult))
            self.evac_post("vector")
            E.op("vector", lambda e, m=m: e.tensor_tensor(
                out=A[5][:], in0=self.KR[m % 2][:], in1=A[3][:], op=ALU.mult))
            E.op("vector", lambda e, m=m: e.tensor_tensor(
                out=A[6][:], in0=A[5][:], in1=A[4][:], op=ALU.add))
            E.op("vector", lambda e, m=m: e.tensor_tensor(
                out=A[5][:], in0=A[6][:], in1=A[m % 2][:], op=ALU.add))
            vm = E.mark("vector", "VE")
            assert vm == fve(m)
            E.wait("sync", "VE", vm)
            if m >= 1:
                E.wait("sync", "LD", st[m - 1])
            E.op("sync", lambda e, m=m: e.dma_start(
                out=self.ht(self.out_h, m), in_=A[5][:]), inc=("LD", 16))
            st[m] = E.count("LD")
            if m + 2 < NM:
                fin_load(m + 2)

        fin_load(0)
        fin_load(1)
        self.cm_te = self.weight_phase(
            "cwv", lambda g, n: self.Yg(g, n * 512, (n + 1) * 512),
            evac_cm,
            m0_per_g=(lambda g: [k2ld[g]]) if "cwv" in PACE else None,
            rhs_waits=() if "cwv" in PACE else tuple(k2gw),
            pre_slices=cwv_pre)

    def phase_final(self):
        E = self.E
        for s in ("LD", "LB0", "LB1", "LW", "LV"):
            E.wait("sync", s, E.count(s))
        for s in ("GW", "GK", "GN", "GV", "GX", "GL", "W0", "W1"):
            E.wait("gpsimd", s, E.count(s))

    def build(self, stop_after=None):
        phases = [("init", self.init_consts)] + \
            ([] if SKIP_P0 else [("p0", self.phase0)]) + \
            [("pk", self.phase_k), ("pv", self.phase_v),
             ("pr", self.phase_r), ("att", self.phase_att),
             ("ffn", self.phase_ffn), ("cm", self.phase_cm),
             ("fin", self.phase_final)]
        with ExitStack() as ctx:
            self.alloc(ctx)
            for name, fn in phases:
                fn()
                if stop_after is not None and name == stop_after:
                    break
            if stop_after is not None and stop_after != "fin":
                for s in ("LD", "LB0", "LB1", "LW", "LV"):
                    self.E.wait("sync", s, self.E.count(s))
                for s in ("GW", "GK", "GN", "GV", "GX", "GL", "W0", "W1"):
                    self.E.wait("gpsimd", s, self.E.count(s))
            self.E.replay(self.nc)
        return self.nc


def build_program(dump=False, stop_after=None):
    return KB(dump=dump).build(stop_after=stop_after)


def make_in_maps(inputs):
    x = np.ascontiguousarray(np.asarray(inputs["L0_attn_x"], np.float32))
    vecs = [
        inputs["L0_attn_ln1_w"], inputs["L0_attn_ln1_b"],
        inputs["L0_attn_ln2_w"], inputs["L0_attn_ln2_b"],
        inputs["L0_attn_tm_u"], inputs["L0_attn_tm_w"],
        inputs["L0_attn_tm_mix_k"], inputs["L0_attn_tm_mix_v"],
        inputs["L0_attn_tm_mix_r"], inputs["L0_attn_tm_bo"],
        inputs["L0_attn_cm_mix_k"], inputs["L0_attn_cm_mix_r"],
        inputs["L0_ffn_scale"],
    ]
    pv = np.ascontiguousarray(
        np.stack([np.asarray(v, np.float32).reshape(16, 128).T for v in vecs],
                 axis=1).reshape(128, len(PVN) * 16))
    wT = {
        "wk": np.ascontiguousarray(np.asarray(inputs["L0_attn_tm_Wk"], np.float32).T),
        "wv": np.ascontiguousarray(np.asarray(inputs["L0_attn_tm_Wv"], np.float32).T),
        "wr": np.ascontiguousarray(np.asarray(inputs["L0_attn_tm_Wr"], np.float32).T),
        "wo": np.ascontiguousarray(np.asarray(inputs["L0_attn_tm_Wo"], np.float32).T),
        "cwk": np.ascontiguousarray(np.asarray(inputs["L0_attn_cm_Wk"], np.float32).T),
        "cwv": np.ascontiguousarray(np.asarray(inputs["L0_attn_cm_Wv"], np.float32).T),
        "cwr": np.ascontiguousarray(np.asarray(inputs["L0_attn_cm_Wr"], np.float32).T),
        "wf": np.ascontiguousarray(np.asarray(inputs["L0_ffn_weight"], np.float32)),
    }
    in_maps = []
    for b in range(NCORES):
        m = {"xt": np.ascontiguousarray(x[b].T), "pv": pv}
        m.update(wT)
        in_maps.append(m)
    return in_maps


def kernel(**inputs):
    nc = build_program(dump=False)
    in_maps = make_in_maps(inputs)
    res = run_bass_kernel_spmd(nc, in_maps, core_ids=list(range(NCORES)))
    out = np.stack([np.ascontiguousarray(np.asarray(r["out"]).T)
                    for r in res.results], axis=0)
    return out.astype(np.float32)
